# version 01 (final)
# baseline (speedup 1.0000x reference)
import os
import numpy as np
import contextlib
import concourse.bass as bass
import concourse.mybir as mybir
from concourse.bass_utils import run_bass_kernel_spmd

F32 = mybir.dt.float32
BF16 = mybir.dt.bfloat16
ALU = mybir.AluOpType
AF = mybir.ActivationFunctionType
AX = mybir.AxisListType.X

N_SUB = int(os.environ.get("BASS_NSUB", "6"))

B, S, D = 4, 2048, 2048
HALF = 1024
TC = 8
KD = 16
INTER = 8192
ISL = 64
N_CTRL = 130156
CONV_DIM = 4352
NCH = 34
CHUNK = 256
NCK = 4
HR = 16

SZ_T = 128 * 2048
OFF_G = 0
OFF_U = OFF_G + 8 * SZ_T
OFF_D = OFF_U + 8 * SZ_T
OFF_1U = OFF_D + 8 * SZ_T
OFF_1D = OFF_1U + 8 * SZ_T
OFF_1E = OFF_1D + 8 * SZ_T
SHARD = OFF_1E + 2 * SZ_T

ST_TAIL = NCH * 4
ST_MST = 32 * 128
ST_RST = 16 * 128
ST_SHARD = ST_TAIL + ST_MST + ST_RST


def _tile_lhsT(wt):
    F = wt.shape[1]
    return np.ascontiguousarray(
        wt.reshape(16, 128, F // 128, 128).transpose(2, 1, 0, 3).reshape(F // 128, 128, 2048))


def _host_prep(ins):
    shared = {}
    per_core = [dict() for _ in range(8)]

    n = INTER * D
    t = np.linspace(0.0, N_CTRL - 1.0, n, dtype=np.float64)
    i0 = np.clip(np.floor(t).astype(np.int64), 0, N_CTRL - 2)
    f = (t - i0).astype(np.float32)
    f2 = f * f
    f3 = f2 * f
    alphas = [0.5 * (-f + 2 * f2 - f3), 0.5 * (2.0 - 5 * f2 + 3 * f3),
              0.5 * (f + 4 * f2 - 3 * f3), 0.5 * (-f2 + f3)]
    idx = [np.clip(i0 + k, 0, N_CTRL - 1) for k in (-1, 0, 1, 2)]

    def tileT(a):
        m = a.reshape(INTER, D).T
        return m.reshape(16, 128, 64, 128).transpose(2, 1, 0, 3).reshape(64, 128, 2048)

    def tileD(a):
        return np.ascontiguousarray(a.reshape(D, INTER).T).reshape(64, 128, 2048)

    gcp, ucp, dcp = ins["L0_ffn_gate_cp"], ins["L0_ffn_up_cp"], ins["L0_ffn_down_cp"]
    ga = [tileT(a) for a in alphas]
    gp = [tileT(gcp[ix]) for ix in idx]
    up = [tileT(ucp[ix]) for ix in idx]
    da = [tileD(a) for a in alphas]
    dp = [tileD(dcp[ix]) for ix in idx]
    for c in range(8):
        sl = slice(c * 8, (c + 1) * 8)
        for k in range(4):
            per_core[c][f"sp_ga{k}"] = np.ascontiguousarray(ga[k][sl])
            per_core[c][f"sp_gp{k}"] = np.ascontiguousarray(gp[k][sl])
            per_core[c][f"sp_up{k}"] = np.ascontiguousarray(up[k][sl])
            per_core[c][f"sp_da{k}"] = np.ascontiguousarray(da[k][sl])
            per_core[c][f"sp_dp{k}"] = np.ascontiguousarray(dp[k][sl])

    u1 = _tile_lhsT(ins["L1_ffn_up_w"])
    d1 = np.ascontiguousarray(ins["L1_ffn_down_w"]).reshape(64, 128, 2048)
    e1 = np.ascontiguousarray(ins["L1_ffn_expert_weight"]).reshape(16, 128, 2048)
    for c in range(8):
        per_core[c]["f1u_sh"] = np.ascontiguousarray(u1[c * 8:(c + 1) * 8])
        per_core[c]["f1d_sh"] = np.ascontiguousarray(d1[c * 8:(c + 1) * 8])
        per_core[c]["f1e_sh"] = np.ascontiguousarray(e1[c * 2:(c + 1) * 2])

    ipT = np.ascontiguousarray(ins["L0_attn_in_proj_w"].T)
    shared["ipj_gate"] = _tile_lhsT(ipT[:, 0:4096])
    shared["ipj_hbc"] = _tile_lhsT(ipT[:, 4096:8448])
    dtw = ipT[:, 8448:8480]
    shared["ipj_dt"] = np.ascontiguousarray(
        dtw.reshape(16, 128, 32).transpose(1, 0, 2).reshape(128, 512))
    shared["opj"] = np.ascontiguousarray(ins["L0_attn_out_proj_w"].T)
    cw = ins["L0_attn_conv_w"].reshape(CONV_DIM, 4)
    shared["convw"] = np.ascontiguousarray(cw.reshape(NCH, 128, 4).transpose(1, 0, 2).reshape(128, NCH * 4))
    shared["convb"] = np.ascontiguousarray(ins["L0_attn_conv_b"].reshape(NCH, 128).T)
    shared["alog"] = np.ascontiguousarray(ins["L0_attn_A_log"].reshape(32, 1))
    shared["dtbias"] = np.ascontiguousarray(ins["L0_attn_dt_bias"].reshape(32, 1))
    shared["d_row"] = np.ascontiguousarray(ins["L0_attn_D"].reshape(1, 32))
    shared["normw"] = np.ascontiguousarray(ins["L0_attn_norm_w"].reshape(32, 128).T)

    wq = np.concatenate([ins["L1_attn_W_Q"][h] for h in range(16)], axis=1)
    wk = np.concatenate([ins["L1_attn_W_K"][h] for h in range(16)], axis=1)
    wv = np.concatenate([ins["L1_attn_W_V"][h] for h in range(16)], axis=1)
    shared["wq_t"] = _tile_lhsT(wq)
    shared["wk_t"] = _tile_lhsT(wk)
    shared["wv_t"] = _tile_lhsT(wv)
    shared["wg"] = np.ascontiguousarray(ins["L1_attn_W_G"])
    shared["wo"] = np.ascontiguousarray(ins["L1_attn_W_O"])
    shared["gnw_cols"] = np.ascontiguousarray(ins["L1_attn_gn_weight"].reshape(16, 128).T)
    shared["gnw_row"] = np.ascontiguousarray(ins["L1_attn_gn_weight"].reshape(1, 2048))
    shared["gnb_row"] = np.ascontiguousarray(ins["L1_attn_gn_bias"].reshape(1, 2048))
    shared["f1g_t"] = np.ascontiguousarray(
        ins["L1_ffn_gate_w"].reshape(16, 128, 8).transpose(1, 0, 2).reshape(128, 128))
    shared["f1lng"] = np.ascontiguousarray(ins["L1_ffn_ln_gamma"].reshape(1, 2048))
    shared["f1lnb"] = np.ascontiguousarray(ins["L1_ffn_ln_beta"].reshape(1, 2048))

    d = 128
    half = 64
    xscale = (np.arange(0, d, 2, dtype=np.float64) + 0.4 * d) / (1.4 * d)
    pos = np.arange(S, dtype=np.float64)
    scale = xscale[None, :] ** (pos[:, None] / 512.0)
    inv_freq = 1.0 / 10000.0 ** (np.arange(half, dtype=np.float64) / half)
    sinu = pos[:, None] * inv_freq[None, :]
    sn, cs = np.sin(sinu), np.cos(sinu)
    gammas = 1.0 - np.exp(np.linspace(np.log(1.0 / 32), np.log(1.0 / 512), HR))
    sgn = np.tile(np.array([-1.0, 1.0]), half)

    def dup(a):
        return np.repeat(a, 2, axis=-1)
    CQ = np.empty((HR, d, S), np.float32); SQ = np.empty((HR, d, S), np.float32)
    CK = np.empty((HR, d, S), np.float32); SK = np.empty((HR, d, S), np.float32)
    for h in range(HR):
        g = gammas[h]
        gn = g ** pos
        gi = g ** (-pos)
        CQ[h] = (dup(cs * scale) * gn[:, None]).T
        SQ[h] = (dup(sn * scale) * sgn[None, :] * gn[:, None]).T
        CK[h] = (dup(cs / scale) * gi[:, None]).T
        SK[h] = (dup(sn / scale) * sgn[None, :] * gi[:, None]).T
    for c in range(8):
        hh = c % 2
        ts_ = slice(hh * 1024, (hh + 1) * 1024)
        for nm, arr in (("xq_c", CQ), ("xq_s", SQ), ("xk_c", CK), ("xk_s", SK)):
            per_core[c][nm] = np.ascontiguousarray(
                arr[:, :, ts_].transpose(1, 0, 2).reshape(128, HR * 1024))

    shared["ident"] = np.eye(128, dtype=np.float32)
    m0 = (np.arange(256)[None, :] >= np.arange(128)[:, None]).astype(np.float32)
    m1 = (np.arange(256)[None, :] >= (128 + np.arange(128))[:, None]).astype(np.float32)
    shared["mask_m"] = np.ascontiguousarray(np.concatenate([m0, m1], axis=1))
    shared["mask_r"] = (np.arange(128)[None, :] >= np.arange(128)[:, None]).astype(np.float32)
    x = ins["L0_attn_hidden_states"]
    for c in range(8):
        b, hh = c // 2, c % 2
        per_core[c]["x0"] = np.ascontiguousarray(x[b, hh * 1024:(hh + 1) * 1024, :])
        per_core[c]["wsel"] = np.full((128, 1), 1.0 if hh == 1 else 0.0, np.float32)
    return shared, per_core


ENGS = ("sync", "gpsimd", "scalar", "vector", "tensor")


class Prog:
    def __init__(self):
        self.q = {e: [] for e in ENGS}
        self.cnt = {}
        self.hdl = {}

    def sem(self, handle, name):
        self.cnt[name] = 0
        self.hdl[name] = handle

    def emit(self, eng, fn, inc=None, n=1):
        if inc is not None:
            self.cnt[inc] += n
        self.q[eng].append((fn, self.hdl.get(inc), n))

    def wait(self, eng, sem_name, val=None):
        v = self.cnt[sem_name] if val is None else val
        self.q[eng].append(("W", self.hdl[sem_name], v))

    def val(self, s):
        return self.cnt[s]

    def barrier(self, engines=ENGS, sems=("sl", "sg", "sv", "ss", "st", "sm", "cc",
                                          "ga", "gb", "gc", "go")):
        for e in engines:
            for s in sems:
                if self.cnt.get(s):
                    self.wait(e, s)

    def replay(self, block):
        for e in ENGS:
            def mk(e):
                def body(eng):
                    for a, b_, c_ in self.q[e]:
                        if a == "W":
                            eng.wait_ge(b_, c_)
                        else:
                            r = a(eng)
                            if b_ is not None:
                                r.then_inc(b_, c_)
                return body
            getattr(block, e)(mk(e))


def build_nc():
    nc = bass.Bass("TRN2", target_bir_lowering=False, detect_race_conditions=True)
    P = 128
    dram = {}

    def din(name, shape, dt=F32):
        dram[name] = nc.dram_tensor(name, list(shape), dt, kind="ExternalInput").ap()

    din("x0", [HALF, D])
    out_t = nc.dram_tensor("out", [HALF, D], F32, kind="ExternalOutput").ap()
    dbg_t = nc.dram_tensor("dbg", [128, 8192], F32, kind="ExternalOutput").ap()
    for k in range(4):
        for pref in ("sp_ga", "sp_gp", "sp_up", "sp_da", "sp_dp"):
            din(f"{pref}{k}", [8, 128, 2048])
    din("f1u_sh", [8, 128, 2048]); din("f1d_sh", [8, 128, 2048]); din("f1e_sh", [2, 128, 2048])
    din("ipj_gate", [32, 128, 2048]); din("ipj_hbc", [34, 128, 2048]); din("ipj_dt", [128, 512])
    din("opj", [4096, 2048]); din("convw", [128, NCH * 4]); din("convb", [128, NCH])
    din("alog", [32, 1]); din("dtbias", [32, 1]); din("d_row", [1, 32]); din("normw", [128, 32])
    din("wq_t", [16, 128, 2048]); din("wk_t", [16, 128, 2048]); din("wv_t", [16, 128, 2048])
    din("wg", [2048, 2048]); din("wo", [2048, 2048])
    din("gnw_cols", [128, 16]); din("gnb_row", [1, 2048]); din("gnw_row", [1, 2048])
    din("f1g_t", [128, 128]); din("f1lng", [1, 2048]); din("f1lnb", [1, 2048])
    for nm in ("xq_c", "xq_s", "xk_c", "xk_s"):
        din(nm, [128, HR * 1024])
    din("ident", [128, 128]); din("mask_m", [128, 512]); din("mask_r", [128, 128])
    din("wsel", [128, 1])

    ag_in = nc.dram_tensor("ag_in", [SHARD], BF16).ap()
    ag_out = nc.dram_tensor("ag_out", [8, SHARD], BF16).ap()
    st_in_a = nc.dram_tensor("st_in_a", [128, 102], F32).ap()
    st_out_a = nc.dram_tensor("st_out_a", [2, 128, 102], F32).ap()
    st_in_b = nc.dram_tensor("st_in_b", [128, 32 * 128], F32).ap()
    st_out_b = nc.dram_tensor("st_out_b", [2, 128, 32 * 128], F32).ap()
    st_in_c = nc.dram_tensor("st_in_c", [128, 16 * 128], F32).ap()
    st_out_c = nc.dram_tensor("st_out_c", [2, 128, 16 * 128], F32).ap()
    rb_in = [nc.dram_tensor(f"rb_in{h}", [128, 128], F32).ap() for h in range(16)]
    rb_out = [nc.dram_tensor(f"rb_out{h}", [2, 128, 128], F32).ap() for h in range(16)]

    ctx = contextlib.ExitStack()

    def sbuf(name, shape, dt=F32):
        return ctx.enter_context(nc.sbuf_tensor(name, list(shape), dt))

    R = sbuf("R", [P, TC * D])
    xlnT = sbuf("xlnT", [P, KD * HALF], BF16)
    AR = 43008
    arena = sbuf("arena", [P, AR], BF16)
    arf = sbuf("arf", [P, 3072])
    stats = sbuf("stats", [P, 64])
    srow = sbuf("srow", [1, 288])
    onesr = sbuf("onesr", [1, 128], BF16)
    onesc = sbuf("onesc", [P, 1], BF16)
    halfc = sbuf("halfc", [P, 1])
    identb = sbuf("identb", [P, 128], BF16)
    maskm = sbuf("maskm", [P, 512], BF16)
    maskr = sbuf("maskr", [P, 128], BF16)
    wselc = sbuf("wselc", [P, 1])
    cwcols = sbuf("cwcols", [P, NCH * 4])
    cbcols = sbuf("cbcols", [P, NCH], BF16)
    m32 = sbuf("m32", [32, 8])
    normw_c = sbuf("normw_c", [P, 32])
    gnw_c = sbuf("gnw_c", [P, 16])
    expcs_sb = sbuf("expcs_sb", [P, 256])
    dbc = sbuf("dbc", [P, 32])
    onesr_f = sbuf("onesr_f", [1, 128])
    onescf = sbuf("onescf", [P, 1])
    halfr_f = sbuf("halfr_f", [1, 128])
    mtok = sbuf("mtok", [P, 768])
    ddt = sbuf("ddt", [P, 256])
    eac = sbuf("eac", [P, 256])
    zsc = sbuf("zsc", [P, 8])
    tails_sb = sbuf("tails_sb", [P, 104], BF16)
    pc03_sb = sbuf("pc03_sb", [P, 104], BF16)
    t_use = sbuf("t_use", [P, 104], BF16)
    Tld = sbuf("Tld", [P, 128], BF16)
    identf = sbuf("identf", [32, 32])
    identf128 = sbuf("identf128", [P, 128])
    winb = sbuf("winb", [P, 416], BF16)

    psf = [ctx.enter_context(nc.psum_tensor(f"psf{i}", [P, 512], F32)) for i in range(6)]
    psb = [ctx.enter_context(nc.psum_tensor(f"psb{i}", [P, 1024], BF16)) for i in range(2)]

    p = Prog()
    for sname in ("sl", "sg", "sv", "ss", "st", "so", "cc", "sm", "qa", "qb", "qc", "qd",
                  "ga", "gb", "gc", "go"):
        p.sem(ctx.enter_context(nc.semaphore(sname)), sname)

    def vdrain():
        p.emit("vector", lambda v: v.drain())

    A_LN = 0
    A_IT = 2048
    A_WS = 34816
    A_DW = A_WS
    A_REC_IN = A_IT
    A_REC_W = A_IT + 16384
    A_END = 43008

    p.emit("sync", lambda s: s.dma_start(
        R[:].rearrange("p (c f) -> p c f", c=TC),
        dram["x0"].rearrange("(c p) f -> p c f", p=128)), inc="sl", n=16)
    for nm, dst in (("ident", identb),):
        p.emit("gpsimd", lambda g, nm=nm, dst=dst: g.dma_start(dst[:], dram[nm]), inc="sg", n=16)
    p.emit("sync", lambda s: s.dma_start(identf[:], dram["ident"][0:32, 0:32]), inc="sl", n=16)
    p.emit("sync", lambda s: s.dma_start(identf128[:], dram["ident"]), inc="sl", n=16)
    p.emit("sync", lambda s: s.dma_start(srow[0:1, 256:288], dram["d_row"]), inc="sl", n=16)
    for nm, dst in (("wsel", wselc), ("normw", normw_c), ("gnw_cols", gnw_c)):
        p.emit("sync", lambda s, nm=nm, dst=dst: s.dma_start(dst[:], dram[nm]), inc="sl", n=16)
    p.emit("sync", lambda s: s.dma_start(cwcols[:], dram["convw"]), inc="sl", n=16)
    for nm, dst in (("mask_m", maskm), ("mask_r", maskr), ("convb", cbcols)):
        p.emit("gpsimd", lambda g, nm=nm, dst=dst: g.dma_start(dst[:], dram[nm]), inc="sg", n=16)
    p.emit("sync", lambda s: s.dma_start(m32[0:32, 0:1], dram["alog"]), inc="sl", n=16)
    p.emit("sync", lambda s: s.dma_start(m32[0:32, 1:2], dram["dtbias"]), inc="sl", n=16)
    p.emit("gpsimd", lambda g: g.memset(onesr[:], 1.0), inc="sm")
    p.emit("gpsimd", lambda g: g.memset(onesc[:], 1.0), inc="sm")
    p.emit("gpsimd", lambda g: g.memset(halfc[:], 0.5), inc="sm")
    p.emit("gpsimd", lambda g: g.memset(onesr_f[:], 1.0), inc="sm")
    p.emit("gpsimd", lambda g: g.memset(onescf[:], 1.0), inc="sm")
    p.emit("gpsimd", lambda g: g.memset(halfr_f[:], 0.5), inc="sm")
    p.emit("gpsimd", lambda g: g.memset(zsc[:], 0.0), inc="sm")

    def recon_weight(a_pref, p_pref, out_off):
        for ti in range(8):
            for k in range(4):
                p.emit("gpsimd", lambda g, k=k, ti=ti: g.dma_start(
                    arena[:, A_REC_IN + k * 2048:A_REC_IN + (k + 1) * 2048],
                    dram[f"{a_pref}{k}"][ti]), inc="sg", n=16)
                p.emit("gpsimd", lambda g, k=k, ti=ti: g.dma_start(
                    arena[:, A_REC_IN + (4 + k) * 2048:A_REC_IN + (5 + k) * 2048],
                    dram[f"{p_pref}{k}"][ti]), inc="sg", n=16)
            p.wait("vector", "sg")

            def mul(v, i, j, o):
                return v.tensor_tensor(
                    arena[:, A_REC_W + o * 2048:A_REC_W + (o + 1) * 2048],
                    arena[:, A_REC_IN + i * 2048:A_REC_IN + (i + 1) * 2048],
                    arena[:, A_REC_IN + j * 2048:A_REC_IN + (j + 1) * 2048], ALU.mult)

            def add(v, i, j, o):
                return v.tensor_tensor(
                    arena[:, A_REC_W + o * 2048:A_REC_W + (o + 1) * 2048],
                    arena[:, A_REC_W + i * 2048:A_REC_W + (i + 1) * 2048],
                    arena[:, A_REC_W + j * 2048:A_REC_W + (j + 1) * 2048], ALU.add)
            p.emit("vector", lambda v: mul(v, 0, 4, 0))
            p.emit("vector", lambda v: mul(v, 1, 5, 1))
            p.emit("vector", lambda v: mul(v, 2, 6, 2))
            p.emit("vector", lambda v: mul(v, 3, 7, 3))
            vdrain()
            p.emit("vector", lambda v: add(v, 0, 1, 4))
            p.emit("vector", lambda v: add(v, 2, 3, 5))
            vdrain()
            p.emit("vector", lambda v: v.tensor_tensor(
                arena[:, A_REC_W + 4 * 2048:A_REC_W + 5 * 2048],
                arena[:, A_REC_W + 4 * 2048:A_REC_W + 5 * 2048],
                arena[:, A_REC_W + 5 * 2048:A_REC_W + 6 * 2048], ALU.add), inc="sv")
            p.wait("sync", "sv")
            o0 = out_off + ti * SZ_T
            p.emit("sync", lambda s, o0=o0: s.dma_start(
                ag_in[o0:o0 + SZ_T].rearrange("(q x) -> q x", q=128),
                arena[:, A_REC_W + 4 * 2048:A_REC_W + 5 * 2048]), inc="so", n=16)
            p.wait("gpsimd", "so")

    def precast(src, ntiles, out_off):
        for ti in range(ntiles):
            p.emit("gpsimd", lambda g, ti=ti: g.dma_start(
                arena[:, A_REC_W + 5 * 2048:A_REC_W + 6 * 2048], dram[src][ti]), inc="sg", n=16)
            p.wait("sync", "sg")
            o0 = out_off + ti * SZ_T
            p.emit("sync", lambda s, o0=o0: s.dma_start(
                ag_in[o0:o0 + SZ_T].rearrange("(q x) -> q x", q=128),
                arena[:, A_REC_W + 5 * 2048:A_REC_W + 6 * 2048]), inc="so", n=16)
            p.wait("gpsimd", "so")

    if N_SUB >= 1:
        recon_weight("sp_ga", "sp_gp", OFF_G)
        recon_weight("sp_ga", "sp_up", OFF_U)
        recon_weight("sp_da", "sp_dp", OFF_D)
        precast("f1u_sh", 8, OFF_1U)
        precast("f1d_sh", 8, OFF_1D)
        precast("f1e_sh", 2, OFF_1E)
        p.wait("gpsimd", "so")
        p.emit("gpsimd", lambda g: g.collective_compute(
            "AllGather", ALU.bypass, replica_groups=[list(range(8))],
            ins=[ag_in.opt()], outs=[ag_out.opt()]), inc="cc")
    CC_W = p.val("cc")

    def ln_phase():
        p.barrier()
        for tc in range(TC):
            p.emit("vector", lambda v, tc=tc: v.tensor_reduce(
                stats[:, tc:tc + 1], R[:, tc * D:(tc + 1) * D], axis=AX, op=ALU.add))
            if tc > 0:
                p.wait("scalar", "sv")
            p.emit("scalar", lambda s, tc=tc: s.activation(
                arf[:, 0:2048], R[:, tc * D:(tc + 1) * D], AF.Square), inc="ss")
            p.wait("vector", "ss")
            p.emit("vector", lambda v, tc=tc: v.tensor_scalar(
                arena[:, A_LN:A_LN + 2048], arf[:, 0:2048],
                1.0 / D, None, ALU.mult, ALU.add, accum_out=stats[:, 8 + tc:9 + tc]), inc="sv")
        vdrain()
        p.emit("vector", lambda v: v.tensor_scalar(
            stats[:, 16:24], stats[:, 0:8], 1.0 / D, None, ALU.mult, ALU.bypass))
        vdrain()
        p.emit("vector", lambda v: v.tensor_tensor(
            stats[:, 32:40], stats[:, 16:24], stats[:, 16:24], ALU.mult))
        vdrain()
        p.emit("vector", lambda v: v.tensor_tensor(
            stats[:, 40:48], stats[:, 8:16], stats[:, 32:40], ALU.subtract))
        vdrain()
        p.emit("vector", lambda v: v.tensor_scalar(
            stats[:, 48:56], stats[:, 40:48], 1e-5, None, ALU.add, ALU.bypass), inc="sv")
        p.wait("scalar", "sv")
        p.emit("scalar", lambda s: s.activation(stats[:, 56:64], stats[:, 48:56], AF.Sqrt), inc="ss")
        p.wait("vector", "ss")
        p.emit("vector", lambda v: v.reciprocal(stats[:, 24:32], stats[:, 56:64]))
        vdrain()
        for tc in range(TC):
            p.emit("vector", lambda v, tc=tc: v.tensor_scalar(
                arena[:, A_LN:A_LN + 2048], R[:, tc * D:(tc + 1) * D],
                stats[:, 16 + tc:17 + tc], stats[:, 24 + tc:25 + tc],
                ALU.subtract, ALU.mult), inc="sv")
            p.wait("tensor", "sv")
            for k in range(KD):
                bank = psb[k % 2]
                p.wait("tensor", "sv", p.val("sv"))
                p.emit("tensor", lambda t, k=k, bank=bank: t.transpose(
                    bank[:, 0:128], arena[:, A_LN + k * 128:A_LN + (k + 1) * 128], identb[:]),
                    inc="st")
                p.wait("vector", "st")
                p.emit("vector", lambda v, k=k, tc=tc, bank=bank: v.tensor_copy(
                    xlnT[:, k * HALF + tc * 128:k * HALF + (tc + 1) * 128], bank[:, 0:128]),
                    inc="sv")

    def ffn_gu(hf, goff, uoff, kind):
        tok0 = hf * 512
        slab_done = [0, 0]
        evac_done = [0, 0]
        sil_done = [0, 0]
        p.barrier()
        for isl in range(ISL):
            slot = isl % 2
            wof = A_WS + slot * 4096
            r, loc = isl // 8, isl % 8
            qs = "qa" if slot == 0 else "qb"
            if isl >= 2:
                p.wait("sync", "st", slab_done[slot])
                p.wait("tensor", "sv", evac_done[slot])
            p.emit("sync", lambda s, r=r, loc=loc, wof=wof: s.dma_start(
                arena[:, wof:wof + 2048],
                ag_out[r, goff + loc * SZ_T:goff + (loc + 1) * SZ_T].rearrange("(q x) -> q x", q=128)),
                inc=qs, n=16)
            if kind == "ffn0":
                p.emit("sync", lambda s, r=r, loc=loc, wof=wof: s.dma_start(
                    arena[:, wof + 2048:wof + 4096],
                    ag_out[r, uoff + loc * SZ_T:uoff + (loc + 1) * SZ_T].rearrange("(q x) -> q x", q=128)),
                    inc=qs, n=16)
            p.wait("tensor", qs)
            bg, bu = psf[slot], psf[2 + slot]
            for k in range(KD):
                last = (k == KD - 1) and (kind != "ffn0")
                p.emit("tensor", lambda t, k=k, wof=wof, bg=bg: t.matmul(
                    bg[:, 0:512], arena[:, wof + k * 128:wof + (k + 1) * 128],
                    xlnT[:, k * HALF + tok0:k * HALF + tok0 + 512],
                    start=(k == 0), stop=(k == KD - 1)), inc=("st" if last else None))
            if kind == "ffn0":
                for k in range(KD):
                    last = (k == KD - 1)
                    p.emit("tensor", lambda t, k=k, wof=wof, bu=bu: t.matmul(
                        bu[:, 0:512], arena[:, wof + 2048 + k * 128:wof + 2048 + (k + 1) * 128],
                        xlnT[:, k * HALF + tok0:k * HALF + tok0 + 512],
                        start=(k == 0), stop=(k == KD - 1)), inc=("st" if last else None))
            slab_done[slot] = p.val("st")
            itsl_off = A_IT + isl * 512
            sof = 2048 + slot * 512
            p.wait("scalar", "st")
            if isl >= 2:
                p.wait("scalar", "sv", sil_done[slot])
            if kind == "ffn0":
                p.emit("scalar", lambda s, bg=bg, sof=sof: s.activation(
                    arf[:, sof:sof + 512], bg[:, 0:512], AF.Silu), inc="ss")
                p.wait("vector", "ss")
                p.emit("vector", lambda v, bu=bu, itsl_off=itsl_off, sof=sof: v.tensor_tensor(
                    arena[:, itsl_off:itsl_off + 512], arf[:, sof:sof + 512],
                    bu[:, 0:512], ALU.mult), inc="sv")
            else:
                p.emit("scalar", lambda s, bg=bg, sof=sof: s.activation(
                    arf[:, sof:sof + 512], bg[:, 0:512], AF.Relu), inc="ss")
                p.wait("vector", "ss")
                p.emit("vector", lambda v, itsl_off=itsl_off, sof=sof: v.tensor_tensor(
                    arena[:, itsl_off:itsl_off + 512], arf[:, sof:sof + 512],
                    arf[:, sof:sof + 512], ALU.mult), inc="sv")
            evac_done[slot] = p.val("sv")
            sil_done[slot] = p.val("sv")

    def ffn_down_half(hf, doff, dest):
        p.barrier()
        slab_done = [0, 0]
        for ns in range(4):
            if ns > 0:
                p.wait("tensor", "sv", p.val("sv"))
            for rg in range(8):
                slot = (ns * 8 + rg) % 2
                dwf = A_DW + slot * 4096
                qs = "qc" if slot == 0 else "qd"
                if ns * 8 + rg >= 2:
                    p.wait("sync", "st", slab_done[slot])
                p.emit("sync", lambda s, rg=rg, ns=ns, dwf=dwf: s.dma_start(
                    arena[:, dwf:dwf + 4096].rearrange("p (a x) -> p a x", a=8),
                    ag_out[rg, doff:doff + 8 * SZ_T].rearrange(
                        "(a q dd) -> q a dd", a=8, q=128)[:, :, ns * 512:(ns + 1) * 512]),
                    inc=qs, n=16)
                p.wait("tensor", qs)
                for tq in range(4):
                    bank = psf[2 + tq]
                    for loc in range(8):
                        ic = rg * 8 + loc
                        last = (rg == 7 and loc == 7)
                        bump = (tq == 3 and loc == 7) or last
                        p.emit("tensor", lambda t, ic=ic, loc=loc, dwf=dwf, bank=bank, tq=tq: t.matmul(
                            bank[:, 0:512],
                            arena[:, A_IT + ic * 512 + tq * 128:A_IT + ic * 512 + (tq + 1) * 128],
                            arena[:, dwf + loc * 512:dwf + (loc + 1) * 512],
                            start=(ic == 0), stop=last, skip_group_check=True),
                            inc=("st" if bump else None))
                slab_done[slot] = p.val("st")
            for tq in range(4):
                dest(hf * 4 + tq, ns, psf[2 + tq])

    def ffn0_call():
        ln_phase()
        p.barrier()
        for hf in range(2):
            ffn_gu(hf, OFF_G, OFF_U, "ffn0")

            def dest(tc, ns, bank):
                p.wait("vector", "st")
                p.emit("vector", lambda v, tc=tc, ns=ns, bank=bank: v.scalar_tensor_tensor(
                    R[:, tc * D + ns * 512:tc * D + (ns + 1) * 512], bank[:, 0:512],
                    halfc[:], R[:, tc * D + ns * 512:tc * D + (ns + 1) * 512],
                    ALU.mult, ALU.add), inc="sv")
            ffn_down_half(hf, OFF_D, dest)

    def store_out():
        p.barrier(engines=("sync",))
        p.emit("sync", lambda s: s.dma_start(
            out_t.rearrange("(c p) f -> p c f", p=128),
            R[:].rearrange("p (c f) -> p c f", c=TC)), inc="so", n=16)
        p.wait("gpsimd", "so")
        p.wait("sync", "so")


    A_HBC = 2048
    A_PC = 36864
    A_MW = 38912
    A_G4 = 36864
    A_MT = 38912
    A_XS = 39936
    A_BT = 40448
    A_BD = 41472
    A_S = 41728
    A_YO = 41984
    A_YD = 42240
    A_YT = 42496
    A_TS = 1024
    A_TC = 1152
    A_YSP = 0
    A_SIL = 1024
    A_PB = 36864
    A_OP = 40960

    def yslot(h):
        return A_YSP if h == 0 else A_HBC + (h - 1) * 1024

    def mamba_phase():
        ln_phase()
        p.barrier()
        evac_done = [0, 0]
        for fc in range(NCH):
            slot = fc % 2
            qs = "ga" if slot == 0 else "gb"
            wof = A_MW + slot * 2048
            pcf = A_PC + slot * 1024
            if fc >= 2:
                p.wait("gpsimd", "st", p.val("st"))
                p.wait("gpsimd", "sv", evac_done[slot])
            p.emit("gpsimd", lambda g, fc=fc, wof=wof: g.dma_start(
                arena[:, wof:wof + 2048], dram["ipj_hbc"][fc]), inc=qs, n=16)
            p.wait("tensor", qs)
            if fc >= 2:
                p.wait("tensor", "sv", evac_done[slot])
            b0, b1 = (psf[0], psf[1]) if slot == 0 else (psf[2], psf[3])
            for k in range(KD):
                p.emit("tensor", lambda t, k=k, wof=wof, b0=b0: t.matmul(
                    b0[:, 0:512], arena[:, wof + k * 128:wof + (k + 1) * 128],
                    xlnT[:, k * HALF:k * HALF + 512],
                    start=(k == 0), stop=(k == KD - 1)))
            for k in range(KD):
                p.emit("tensor", lambda t, k=k, wof=wof, b1=b1: t.matmul(
                    b1[:, 0:512], arena[:, wof + k * 128:wof + (k + 1) * 128],
                    xlnT[:, k * HALF + 512:k * HALF + 1024],
                    start=(k == 0), stop=(k == KD - 1)), inc=("st" if k == KD - 1 else None))
            p.wait("vector", "st")
            p.emit("vector", lambda v, b0=b0, pcf=pcf: v.tensor_copy(
                arena[:, pcf:pcf + 512], b0[:, 0:512]))
            p.emit("vector", lambda v, b1=b1, pcf=pcf: v.tensor_copy(
                arena[:, pcf + 512:pcf + 1024], b1[:, 0:512]))
            p.emit("vector", lambda v, b0=b0, fc=fc: v.tensor_copy(
                pc03_sb[:, fc * 3:fc * 3 + 3], b0[:, 0:3]))
            p.emit("vector", lambda v, b1=b1, fc=fc: v.tensor_copy(
                tails_sb[:, fc * 3:fc * 3 + 3], b1[:, 509:512]), inc="sv")
            evac_done[slot] = p.val("sv")
            vdrain()
            p.emit("vector", lambda v, pcf=pcf, fc=fc: v.tensor_scalar(
                arf[:, 0:1021], arena[:, pcf:pcf + 1021],
                cwcols[:, fc * 4:fc * 4 + 1], None, ALU.mult, ALU.bypass))
            vdrain()
            p.emit("vector", lambda v, pcf=pcf, fc=fc: v.scalar_tensor_tensor(
                arf[:, 1024:1024 + 1021], arena[:, pcf + 1:pcf + 1 + 1021],
                cwcols[:, fc * 4 + 1:fc * 4 + 2], arf[:, 0:1021], ALU.mult, ALU.add))
            vdrain()
            p.emit("vector", lambda v, pcf=pcf, fc=fc: v.scalar_tensor_tensor(
                arf[:, 0:1021], arena[:, pcf + 2:pcf + 2 + 1021],
                cwcols[:, fc * 4 + 2:fc * 4 + 3], arf[:, 1024:1024 + 1021], ALU.mult, ALU.add))
            vdrain()
            p.emit("vector", lambda v, pcf=pcf, fc=fc: v.scalar_tensor_tensor(
                arf[:, 1024:1024 + 1021], arena[:, pcf + 3:pcf + 3 + 1021],
                cwcols[:, fc * 4 + 3:fc * 4 + 4], arf[:, 0:1021], ALU.mult, ALU.add), inc="sv")
            p.wait("scalar", "sv")
            p.emit("scalar", lambda s, fc=fc: s.activation(
                arena[:, A_HBC + fc * 1024 + 3:A_HBC + (fc + 1) * 1024],
                arf[:, 1024:1024 + 1021], AF.Silu,
                bias=cbcols[:, fc:fc + 1]), inc="ss")
            p.wait("vector", "ss")
        p.barrier()
        p.emit("gpsimd", lambda g: g.dma_start(st_in_a[:, 0:102], tails_sb[:, 0:102]), inc="go", n=16)
        p.wait("gpsimd", "go")
        p.emit("gpsimd", lambda g: g.collective_compute(
            "AllGather", ALU.bypass, replica_groups=[[0, 1], [2, 3], [4, 5], [6, 7]],
            ins=[st_in_a.opt()], outs=[st_out_a.opt()]), inc="cc")
        CC_A = p.val("cc")

        p.emit("gpsimd", lambda g: g.dma_start(
            arena[:, A_MW:A_MW + 512], dram["ipj_dt"]), inc="sg", n=16)
        p.barrier()
        for half_i in range(2):
            bank = psf[4] if half_i == 0 else psf[5]
            for k in range(KD):
                p.emit("tensor", lambda t, k=k, bank=bank, half_i=half_i: t.matmul(
                    bank[0:32, 0:512], arena[:, A_MW + k * 32:A_MW + (k + 1) * 32],
                    xlnT[:, k * HALF + half_i * 512:k * HALF + (half_i + 1) * 512],
                    start=(k == 0), stop=(k == KD - 1)), inc=("st" if k == KD - 1 else None))
        p.wait("vector", "st")
        p.emit("vector", lambda v: v.tensor_copy(arf[0:32, 0:512], psf[4][0:32, 0:512]))
        p.emit("vector", lambda v: v.tensor_copy(arf[0:32, 512:1024], psf[5][0:32, 0:512]), inc="sv")
        p.wait("scalar", "sv")
        p.emit("scalar", lambda s: s.activation(
            arf[0:32, 1024:2048], arf[0:32, 0:1024], AF.Exp,
            bias=m32[0:32, 1:2]), inc="ss")
        p.emit("scalar", lambda s: s.drain())
        p.emit("scalar", lambda s: s.activation(
            arf[0:32, 1024:2048], arf[0:32, 1024:2048], AF.Ln, bias=1.0), inc="ss")
        p.emit("scalar", lambda s: s.activation(m32[0:32, 2:3], m32[0:32, 0:1], AF.Exp), inc="ss")
        p.wait("vector", "ss")
        p.emit("vector", lambda v: v.tensor_scalar(
            arf[0:32, 0:1024], arf[0:32, 1024:2048], 100.0, None, ALU.min, ALU.bypass))
        p.emit("vector", lambda v: v.tensor_scalar(
            m32[0:32, 3:4], m32[0:32, 2:3], -1.0, None, ALU.mult, ALU.bypass))
        vdrain()
        p.emit("vector", lambda v: v.tensor_scalar(
            arf[0:32, 1024:2048], arf[0:32, 0:1024], m32[0:32, 3:4], None, ALU.mult, ALU.bypass))
        vdrain()
        for c in range(NCK):
            p.emit("vector", lambda v, c=c: v.tensor_tensor_scan(
                arf[0:32, 2048 + c * 256:2048 + (c + 1) * 256],
                arf[0:32, 1024 + c * 256:1024 + (c + 1) * 256],
                zsc[0:32, 0:1].broadcast_to([32, 256]), 0.0, ALU.add, ALU.add))
        vdrain()
        p.emit("vector", lambda v: v.tensor_reduce(
            m32[0:32, 4:8], arf[0:32, 1024:2048].rearrange("p (c l) -> p c l", c=4),
            axis=AX, op=ALU.add))
        vdrain()
        p.emit("vector", lambda v: v.tensor_copy(stats[0:32, 40:44], m32[0:32, 4:8]), inc="sv")
        p.wait("tensor", "sv")
        for b in range(8):
            p.emit("tensor", lambda t, b=b: t.transpose(
                psf[4][:, 0:32], arf[0:32, b * 128:(b + 1) * 128], identf[:]))
            p.emit("tensor", lambda t, b=b: t.transpose(
                psf[4][:, 32:64], arf[0:32, 2048 + b * 128:2048 + (b + 1) * 128], identf[:]),
                inc="st")
            p.wait("vector", "st")
            p.emit("vector", lambda v, b=b: v.tensor_copy(
                mtok[:, b * 96:b * 96 + 64], psf[4][:, 0:64]), inc="sv")
            p.wait("tensor", "sv")
        for c in range(NCK):
            p.emit("vector", lambda v, c=c: v.tensor_scalar(
                arf[0:32, c * 256:(c + 1) * 256],
                arf[0:32, 2048 + c * 256:2048 + (c + 1) * 256],
                m32[0:32, 4 + c:5 + c], None, ALU.subtract, ALU.bypass))
        vdrain()
        p.emit("vector", lambda v: v.tensor_copy(stats[0:32, 44:45], m32[0:32, 4:5]), inc="sv")
        p.wait("tensor", "sv")
        for b in range(8):
            p.emit("tensor", lambda t, b=b: t.transpose(
                psf[4][:, 64:96], arf[0:32, b * 128:(b + 1) * 128], identf[:]), inc="st")
            p.wait("vector", "st")
            p.emit("vector", lambda v, b=b: v.tensor_copy(
                mtok[:, b * 96 + 64:b * 96 + 96], psf[4][:, 64:96]), inc="sv")
            p.wait("tensor", "sv")
        for c in range(NCK):
            p.emit("tensor", lambda t, c=c: t.transpose(
                psf[5][0:1, c * 32:(c + 1) * 32], m32[0:32, 4 + c:5 + c], identf[:]),
                inc=("st" if c == NCK - 1 else None))
        p.wait("vector", "st")
        p.emit("vector", lambda v: v.tensor_copy(srow[0:1, 0:128], psf[5][0:1, 0:128]), inc="sv")
        p.wait("scalar", "sv")
        p.emit("scalar", lambda s: s.activation(srow[0:1, 128:256], srow[0:1, 0:128], AF.Exp), inc="ss")
        p.wait("tensor", "ss")
        p.emit("tensor", lambda t: t.matmul(
            psf[5][:, 0:128], onesr_f[:], srow[0:1, 128:256], start=True, stop=True), inc="st")
        p.emit("tensor", lambda t: t.matmul(
            psf[4][:, 0:32], onesr_f[:], srow[0:1, 256:288], start=True, stop=True), inc="st")
        p.wait("vector", "st")
        p.emit("vector", lambda v: v.tensor_copy(expcs_sb[:, 0:128], psf[5][:, 0:128]))
        p.emit("vector", lambda v: v.tensor_copy(dbc[:, 0:32], psf[4][:, 0:32]))
        vdrain()
        p.emit("vector", lambda v: v.tensor_scalar(
            expcs_sb[:, 128:160], expcs_sb[:, 0:32], 0.0, 1.0, ALU.mult, ALU.add))
        vdrain()
        for c in range(1, NCK):
            p.emit("vector", lambda v, c=c: v.tensor_tensor(
                expcs_sb[:, 128 + c * 32:128 + (c + 1) * 32],
                expcs_sb[:, 128 + (c - 1) * 32:128 + c * 32],
                expcs_sb[:, (c - 1) * 32:c * 32], ALU.mult))
            vdrain()
        p.wait("scalar", "sv", p.val("sv"))
        m3 = mtok[:].rearrange("p (b q) -> p b q", b=8)
        d3 = ddt[:].rearrange("p (b q) -> p b q", b=8)
        e3 = eac[:].rearrange("p (b q) -> p b q", b=8)
        p.emit("scalar", lambda s: s.activation(d3, m3[:, :, 64:96], AF.Exp, scale=-1.0), inc="ss")
        p.emit("scalar", lambda s: s.activation(e3, m3[:, :, 32:64], AF.Exp), inc="ss")
        p.wait("vector", "ss")
        p.emit("vector", lambda v: v.tensor_tensor(d3, d3, m3[:, :, 0:32], ALU.mult), inc="sv")
        vdrain()
        p.wait("sync", "cc", CC_A)
        p.emit("gpsimd", lambda g: g.dma_start(tails_sb[:, 0:102], st_out_a[0, :, 0:102]), inc="sg", n=16)
        p.wait("vector", "sg")
        p.emit("vector", lambda v: v.tensor_scalar(
            t_use[:, 0:102], tails_sb[:, 0:102], wselc[:], None, ALU.mult, ALU.bypass))
        vdrain()
        w3o = winb[:, 0:204].rearrange("p (a j) -> p a j", a=NCH)
        p.emit("vector", lambda v: v.tensor_copy(
            w3o[:, :, 0:3], t_use[:, 0:102].rearrange("p (a j) -> p a j", a=NCH)))
        p.emit("vector", lambda v: v.tensor_copy(
            w3o[:, :, 3:6], pc03_sb[:, 0:102].rearrange("p (a j) -> p a j", a=NCH)))
        vdrain()
        cw3 = cwcols[:].rearrange("p (a j) -> p a j", a=NCH)
        accA = winb[:, 208:310].rearrange("p (a j) -> p a j", a=NCH)
        accB = winb[:, 312:414].rearrange("p (a j) -> p a j", a=NCH)
        for k in range(4):
            w3 = winb[:, k:k + 202].rearrange("p (a j) -> p a j", a=NCH, j=6)[:, :, 0:3] \
                if False else winb[:, 0:208][:, k:k + 204].rearrange("p (a j) -> p a j", a=NCH)[:, :, 0:3]
            c3 = cw3[:, :, k:k + 1].broadcast_to([128, NCH, 3])
            dst = accA if k == 0 else accB
            p.emit("vector", lambda v, w3=w3, c3=c3, dst=dst: v.tensor_tensor(dst, w3, c3, ALU.mult))
            if k > 0:
                vdrain()
                p.emit("vector", lambda v: v.tensor_tensor(
                    winb[:, 208:310], winb[:, 208:310], winb[:, 312:414], ALU.add))
            vdrain()
        cb3 = cbcols[:].rearrange("p (a j) -> p a j", j=1)[:, :, 0:1].broadcast_to([128, NCH, 3])
        p.emit("vector", lambda v, cb3=cb3: v.tensor_tensor(accA, accA, cb3, ALU.add), inc="sv")
        p.wait("scalar", "sv")
        p.emit("scalar", lambda s: s.activation(winb[:, 312:414], winb[:, 208:310], AF.Silu), inc="ss")
        p.wait("vector", "ss")
        hb3 = arena[:, A_HBC:A_HBC + NCH * 1024].rearrange("p (a x) -> p a x", a=NCH)
        p.emit("vector", lambda v: v.tensor_copy(
            hb3[:, :, 0:3], winb[:, 312:414].rearrange("p (a j) -> p a j", a=NCH)), inc="sv")
        p.barrier()

        CC_B = [0]
        for sweep in (0, 1):
            if sweep == 1:
                p.wait("gpsimd", "go")
                p.emit("gpsimd", lambda g: g.collective_compute(
                    "AllGather", ALU.bypass, replica_groups=[[0, 1], [2, 3], [4, 5], [6, 7]],
                    ins=[st_in_b.opt()], outs=[st_out_b.opt()]), inc="cc")
                CC_B[0] = p.val("cc")
            for h in range(32):
                hoff = A_HBC + h * 1024
                if sweep == 1:
                    p.wait("gpsimd", "cc", CC_B[0])
                    if h > 0:
                        p.wait("gpsimd", "sv")
                    p.emit("gpsimd", lambda g, h=h: g.dma_start(
                        Tld[:], st_out_b[0, :, h * 128:(h + 1) * 128]), inc="gc", n=16)
                    p.wait("vector", "gc")
                    p.emit("vector", lambda v: v.tensor_scalar(
                        arena[:, A_TS:A_TS + 128], Tld[:], wselc[:], None, ALU.mult, ALU.bypass), inc="sv")
                vdrain()
                if sweep == 0 and h > 0:
                    p.wait("vector", "go")
                p.emit("vector", lambda v: v.tensor_scalar(
                    arena[:, A_S:A_S + 128], identb[:], 0.0, None, ALU.mult, ALU.bypass), inc="sv")
                for c in range(NCK):
                    scur = A_S + (c % 2) * 128
                    snxt = A_S + ((c + 1) % 2) * 128
                    xsp = A_XS + (c % 2) * 256
                    mts = A_MT + (c % 2) * 512
                    cur_sv = p.val("sv")
                    p.wait("tensor", "sv", cur_sv)
                    for sb_ in range(2):
                        p.emit("tensor", lambda t, hoff=hoff, c=c, sb_=sb_: t.transpose(
                            psb[0][:, sb_ * 128:(sb_ + 1) * 128],
                            arena[:, hoff + c * 256 + sb_ * 128:hoff + c * 256 + (sb_ + 1) * 128],
                            identb[:]), inc=("st" if sb_ == 1 else None))
                    p.wait("vector", "st")
                    for sb_ in range(2):
                        p.emit("vector", lambda v, xsp=xsp, sb_=sb_: v.tensor_copy(
                            arena[:, xsp + sb_ * 128:xsp + (sb_ + 1) * 128],
                            psb[0][:, sb_ * 128:(sb_ + 1) * 128]), inc="sv")
                    if sweep == 0 and h == 0:
                        p.wait("tensor", "sv", p.val("sv"))
                        for sb_ in range(2):
                            p.emit("tensor", lambda t, c=c, sb_=sb_: t.transpose(
                                psb[1][:, sb_ * 128:(sb_ + 1) * 128],
                                arena[:, A_HBC + 32 * 1024 + c * 256 + sb_ * 128:
                                       A_HBC + 32 * 1024 + c * 256 + (sb_ + 1) * 128],
                                identb[:]), inc=("st" if sb_ == 1 else None))
                        p.wait("vector", "st")
                        for sb_ in range(2):
                            p.emit("vector", lambda v, c=c, sb_=sb_: v.tensor_copy(
                                arena[:, A_BT + (c * 2 + sb_) * 128:A_BT + (c * 2 + sb_ + 1) * 128],
                                psb[1][:, sb_ * 128:(sb_ + 1) * 128]), inc="sv")
                    if sweep == 1 and h == 0:
                        p.wait("tensor", "sv", p.val("sv"))
                        for sb_ in range(2):
                            p.emit("tensor", lambda t, c=c, sb_=sb_: t.matmul(
                                psf[0][:, sb_ * 256:(sb_ + 1) * 256],
                                arena[:, A_HBC + 32 * 1024 + c * 256 + sb_ * 128:
                                       A_HBC + 32 * 1024 + c * 256 + (sb_ + 1) * 128],
                                arena[:, A_HBC + 33 * 1024 + c * 256:A_HBC + 33 * 1024 + (c + 1) * 256],
                                start=True, stop=True), inc=("st" if sb_ == 1 else None))
                        p.wait("vector", "st")
                        for sb_ in range(2):
                            p.emit("vector", lambda v, c=c, sb_=sb_: v.tensor_tensor(
                                arena[:, A_G4 + c * 512 + sb_ * 256:A_G4 + c * 512 + (sb_ + 1) * 256],
                                psf[0][:, sb_ * 256:(sb_ + 1) * 256],
                                maskm[:, sb_ * 256:(sb_ + 1) * 256], ALU.mult), inc="sv")
                    p.wait("vector", "sv", p.val("sv"))
                    for sb_ in range(2):
                        p.emit("vector", lambda v, c=c, sb_=sb_, h=h: v.tensor_scalar(
                            arena[:, A_BD + sb_ * 128:A_BD + (sb_ + 1) * 128],
                            arena[:, A_BT + (c * 2 + sb_) * 128:A_BT + (c * 2 + sb_ + 1) * 128],
                            ddt[:, (2 * c + sb_) * 32 + h:(2 * c + sb_) * 32 + h + 1],
                            None, ALU.mult, ALU.bypass), inc="sv")
                    p.wait("tensor", "sv")
                    for sb_ in range(2):
                        p.emit("tensor", lambda t, xsp=xsp, sb_=sb_: t.matmul(
                            psf[1][:, 0:128],
                            arena[:, A_BD + sb_ * 128:A_BD + (sb_ + 1) * 128],
                            arena[:, xsp + sb_ * 128:xsp + (sb_ + 1) * 128],
                            start=(sb_ == 0), stop=(sb_ == 1)), inc=("st" if sb_ == 1 else None))
                    if sweep == 1:
                        for sb_ in range(2):
                            b_ = 2 * c + sb_
                            p.emit("tensor", lambda t, b_=b_, h=h, sb_=sb_: t.transpose(
                                psf[4][0:1, sb_ * 128:(sb_ + 1) * 128],
                                mtok[:, b_ * 96 + 32 + h:b_ * 96 + 32 + h + 1],
                                identf128[:]), inc=("st" if sb_ == 1 else None))
                        p.wait("vector", "st")
                        p.emit("vector", lambda v: v.tensor_copy(
                            srow[0:1, 0:256], psf[4][0:1, 0:256]), inc="sv")
                        p.wait("tensor", "sv")
                        p.emit("tensor", lambda t: t.matmul(
                            psf[4][:, 0:256], onesr_f[:], srow[0:1, 0:256],
                            start=True, stop=True), inc="st")
                        p.wait("vector", "st")
                        for sb_ in range(2):
                            b_ = 2 * c + sb_
                            p.emit("vector", lambda v, sb_=sb_, b_=b_, h=h: v.tensor_scalar(
                                psf[5][:, sb_ * 256:(sb_ + 1) * 256], psf[4][:, 0:256],
                                mtok[:, b_ * 96 + 32 + h:b_ * 96 + 32 + h + 1], 0.0,
                                ALU.subtract, ALU.min), inc="sv")
                        p.wait("scalar", "sv")
                        for sb_ in range(2):
                            p.emit("scalar", lambda s, sb_=sb_: s.activation(
                                arena[:, 1280 + sb_ * 256:1280 + (sb_ + 1) * 256],
                                psf[5][:, sb_ * 256:(sb_ + 1) * 256], AF.Exp), inc="ss")
                        p.wait("vector", "ss")
                        for sb_ in range(2):
                            b_ = 2 * c + sb_
                            p.emit("vector", lambda v, sb_=sb_, b_=b_, h=h, mts=mts, c=c: v.scalar_tensor_tensor(
                                arena[:, mts + sb_ * 256:mts + (sb_ + 1) * 256],
                                arena[:, 1280 + sb_ * 256:1280 + (sb_ + 1) * 256],
                                mtok[:, b_ * 96 + h:b_ * 96 + h + 1],
                                arena[:, A_G4 + c * 512 + sb_ * 256:A_G4 + c * 512 + (sb_ + 1) * 256],
                                ALU.mult, ALU.mult), inc="sv")
                        p.emit("vector", lambda v, c=c, h=h: v.tensor_scalar(
                            arena[:, A_TC:A_TC + 128], arena[:, A_TS:A_TS + 128],
                            expcs_sb[:, 128 + c * 32 + h:128 + c * 32 + h + 1],
                            None, ALU.mult, ALU.bypass), inc="sv")
                        p.wait("tensor", "sv")
                        p.emit("tensor", lambda t, mts=mts, xsp=xsp: t.matmul(
                            psf[2][:, 0:128], arena[:, mts + 0:mts + 128],
                            arena[:, xsp:xsp + 128], start=True, stop=True,
                            skip_group_check=True))
                        p.emit("tensor", lambda t, mts=mts, xsp=xsp: t.matmul(
                            psf[2][:, 128:256], arena[:, mts + 128:mts + 256],
                            arena[:, xsp:xsp + 128], start=True, stop=False,
                            skip_group_check=True))
                        p.emit("tensor", lambda t, mts=mts, xsp=xsp: t.matmul(
                            psf[2][:, 128:256], arena[:, mts + 256 + 128:mts + 512],
                            arena[:, xsp + 128:xsp + 256], start=False, stop=True,
                            skip_group_check=True))
                        for lb in range(2):
                            p.emit("tensor", lambda t, c=c, lb=lb, scur=scur: t.matmul(
                                psf[3][:, lb * 128:(lb + 1) * 128],
                                arena[:, A_HBC + 33 * 1024 + c * 256 + lb * 128:
                                       A_HBC + 33 * 1024 + c * 256 + (lb + 1) * 128],
                                arena[:, scur:scur + 128], start=True, stop=False,
                                skip_group_check=True))
                        for lb in range(2):
                            p.emit("tensor", lambda t, c=c, lb=lb: t.matmul(
                                psf[3][:, lb * 128:(lb + 1) * 128],
                                arena[:, A_HBC + 33 * 1024 + c * 256 + lb * 128:
                                       A_HBC + 33 * 1024 + c * 256 + (lb + 1) * 128],
                                arena[:, A_TC:A_TC + 128], start=False, stop=True,
                                skip_group_check=True), inc=("st" if lb == 1 else None))
                    p.wait("vector", "st")
                    p.emit("vector", lambda v, scur=scur, snxt=snxt, c=c, h=h: v.scalar_tensor_tensor(
                        arena[:, snxt:snxt + 128], arena[:, scur:scur + 128],
                        expcs_sb[:, c * 32 + h:c * 32 + h + 1],
                        psf[1][:, 0:128], ALU.mult, ALU.add), inc="sv")
                    if sweep == 1:
                        for lb in range(2):
                            b_ = 2 * c + lb
                            p.emit("vector", lambda v, lb=lb, b_=b_, h=h: v.tensor_scalar(
                                arena[:, A_YO + lb * 128:A_YO + (lb + 1) * 128],
                                psf[3][:, lb * 128:(lb + 1) * 128],
                                eac[:, b_ * 32 + h:b_ * 32 + h + 1], None,
                                ALU.mult, ALU.bypass))
                            p.emit("vector", lambda v, lb=lb, h=h, xsp=xsp: v.scalar_tensor_tensor(
                                arena[:, A_YD + lb * 128:A_YD + (lb + 1) * 128],
                                arena[:, xsp + lb * 128:xsp + (lb + 1) * 128],
                                dbc[:, h:h + 1],
                                psf[2][:, lb * 128:(lb + 1) * 128], ALU.mult, ALU.add))
                        vdrain()
                        for lb in range(2):
                            p.emit("vector", lambda v, lb=lb: v.tensor_tensor(
                                arena[:, A_YT + lb * 128:A_YT + (lb + 1) * 128],
                                arena[:, A_YO + lb * 128:A_YO + (lb + 1) * 128],
                                arena[:, A_YD + lb * 128:A_YD + (lb + 1) * 128], ALU.add), inc="sv")
                        p.wait("tensor", "sv")
                        for lb in range(2):
                            p.emit("tensor", lambda t, lb=lb: t.transpose(
                                psb[1][:, lb * 128:(lb + 1) * 128],
                                arena[:, A_YT + lb * 128:A_YT + (lb + 1) * 128],
                                identb[:]), inc=("st" if lb == 1 else None))
                        p.wait("vector", "st")
                        ys = yslot(h)
                        for lb in range(2):
                            p.emit("vector", lambda v, ys=ys, c=c, lb=lb: v.tensor_copy(
                                arena[:, ys + c * 256 + lb * 128:ys + c * 256 + (lb + 1) * 128],
                                psb[1][:, lb * 128:(lb + 1) * 128]), inc="sv")
                if sweep == 0:
                    sfin = A_S + (NCK % 2) * 128
                    p.wait("gpsimd", "sv")
                    p.emit("gpsimd", lambda g, h=h, sfin=sfin: g.dma_start(
                        st_in_b[:, h * 128:(h + 1) * 128], arena[:, sfin:sfin + 128]),
                        inc="go", n=16)
        p.barrier()
        for g in range(32):
            slot = g % 2
            qs = "ga" if slot == 0 else "gb"
            wof = A_PB + slot * 2048
            b0, b1 = (psf[0], psf[1]) if slot == 0 else (psf[2], psf[3])
            if g >= 2:
                p.wait("gpsimd", "st", p.val("st"))
                p.wait("gpsimd", "sv", p.val("sv"))
            p.emit("gpsimd", lambda gg, g=g, wof=wof: gg.dma_start(
                arena[:, wof:wof + 2048], dram["ipj_gate"][g]), inc=qs, n=16)
            p.wait("tensor", qs)
            p.wait("tensor", "sv", p.val("sv"))
            for half_i in range(2):
                for k in range(KD):
                    bk = b0 if half_i == 0 else b1
                    p.emit("tensor", lambda t, k=k, wof=wof, bk=bk, half_i=half_i: t.matmul(
                        bk[:, 0:512], arena[:, wof + k * 128:wof + (k + 1) * 128],
                        xlnT[:, k * HALF + half_i * 512:k * HALF + (half_i + 1) * 512],
                        start=(k == 0), stop=(k == KD - 1)),
                        inc=("st" if (k == KD - 1 and half_i == 1) else None))
            p.wait("scalar", "st")
            p.wait("scalar", "sv", p.val("sv"))
            p.emit("scalar", lambda s, b0=b0: s.activation(
                arena[:, A_SIL:A_SIL + 512], b0[:, 0:512], AF.Silu), inc="ss")
            p.emit("scalar", lambda s, b1=b1: s.activation(
                arena[:, A_SIL + 512:A_SIL + 1024], b1[:, 0:512], AF.Silu), inc="ss")
            p.wait("vector", "ss")
            ys = yslot(g)
            p.emit("vector", lambda v, ys=ys: v.tensor_tensor(
                arena[:, ys:ys + 1024], arena[:, ys:ys + 1024],
                arena[:, A_SIL:A_SIL + 1024], ALU.mult), inc="sv")
        p.barrier()
        for g in range(32):
            slot = g % 2
            if g >= 2:
                p.wait("scalar", "st", p.val("st"))
            ys = yslot(g)
            p.emit("scalar", lambda s, ys=ys, slot=slot: s.activation(
                arf[:, slot * 1024:slot * 1024 + 1024], arena[:, ys:ys + 1024], AF.Square),
                inc="ss")
            p.wait("tensor", "ss")
            for half_i in range(2):
                bk = psf[4] if half_i == 0 else psf[5]
                p.emit("tensor", lambda t, g=g, slot=slot, half_i=half_i, bk=bk: t.matmul(
                    bk[0:1, 0:512], onescf[:],
                    arf[:, slot * 1024 + half_i * 512:slot * 1024 + (half_i + 1) * 512],
                    start=(g == 0), stop=(g == 31), skip_group_check=True),
                    inc=("st" if half_i == 1 else None))
        p.wait("vector", "st")
        p.emit("vector", lambda v: v.tensor_scalar(
            arf[0:1, 0:512], psf[4][0:1, 0:512], 1.0 / 4096, 1e-5, ALU.mult, ALU.add))
        p.emit("vector", lambda v: v.tensor_scalar(
            arf[0:1, 512:1024], psf[5][0:1, 0:512], 1.0 / 4096, 1e-5, ALU.mult, ALU.add), inc="sv")
        p.wait("scalar", "sv")
        p.emit("scalar", lambda s: s.activation(
            arf[0:1, 1024:2048], arf[0:1, 0:1024], AF.Sqrt), inc="ss")
        p.wait("vector", "ss")
        p.emit("vector", lambda v: v.reciprocal(arf[0:1, 2048:3072], arf[0:1, 1024:2048]), inc="sv")
        p.wait("tensor", "sv")
        for half_i in range(2):
            bk = psf[4] if half_i == 0 else psf[5]
            p.emit("tensor", lambda t, half_i=half_i, bk=bk: t.matmul(
                bk[:, 0:512], onesr_f[:],
                arf[0:1, 2048 + half_i * 512:2048 + (half_i + 1) * 512],
                start=True, stop=True), inc="st")
        p.wait("vector", "st")
        for g in range(32):
            ys = yslot(g)
            for half_i in range(2):
                bk = psf[4] if half_i == 0 else psf[5]
                p.emit("vector", lambda v, ys=ys, g=g, half_i=half_i, bk=bk: v.scalar_tensor_tensor(
                    arena[:, ys + half_i * 512:ys + (half_i + 1) * 512],
                    arena[:, ys + half_i * 512:ys + (half_i + 1) * 512],
                    normw_c[:, g:g + 1], bk[:, 0:512], ALU.mult, ALU.mult), inc="sv")
            vdrain()
        p.barrier()
        for hf in range(2):
            for ns in range(4):
                if hf + ns > 0:
                    p.wait("tensor", "sv", p.val("sv"))
                for g in range(32):
                    slot = g % 2
                    qs = "ga" if slot == 0 else "gb"
                    wof = A_OP + slot * 512
                    if g >= 2 or hf + ns > 0:
                        p.wait("gpsimd", "st", p.val("st"))
                    p.emit("gpsimd", lambda gg, g=g, ns=ns, wof=wof: gg.dma_start(
                        arena[:, wof:wof + 512],
                        dram["opj"][g * 128:(g + 1) * 128, ns * 512:(ns + 1) * 512]),
                        inc=qs, n=16)
                    p.wait("tensor", qs)
                    for tq in range(4):
                        ys = yslot(g)
                        last = (g == 31)
                        bump = (tq == 3) or (last and tq == 3)
                        p.emit("tensor", lambda t, ys=ys, hf=hf, tq=tq, wof=wof, g=g, last=last: t.matmul(
                            psf[2 + tq][:, 0:512],
                            arena[:, ys + hf * 512 + tq * 128:ys + hf * 512 + (tq + 1) * 128],
                            arena[:, wof:wof + 512],
                            start=(g == 0), stop=last, skip_group_check=True),
                            inc=("st" if tq == 3 else None))
                p.wait("vector", "st")
                for tq in range(4):
                    tc = hf * 4 + tq
                    p.emit("vector", lambda v, tc=tc, ns=ns, tq=tq: v.tensor_tensor(
                        R[:, tc * D + ns * 512:tc * D + (ns + 1) * 512],
                        R[:, tc * D + ns * 512:tc * D + (ns + 1) * 512],
                        psf[2 + tq][:, 0:512], ALU.add), inc="sv")
        p.barrier()


    A_IT1 = 2048
    A_Y1 = 18432
    A_YN = 22528
    A_MW1 = 24576
    A_T1 = 28672
    A_FG = 30720

    def ffn1_call():
        ln_phase()
        p.barrier()
        p.emit("gpsimd", lambda g: g.dma_start(
            arena[:, A_FG:A_FG + 128], dram["f1g_t"]), inc="sg", n=16)
        p.wait("tensor", "sg")
        for tc in range(TC):
            for k in range(KD):
                p.emit("tensor", lambda t, tc=tc, k=k: t.matmul(
                    psf[5][:, 0:8], xlnT[:, k * HALF + tc * 128:k * HALF + (tc + 1) * 128],
                    arena[:, A_FG + k * 8:A_FG + (k + 1) * 8],
                    start=(k == 0), stop=(k == KD - 1)), inc=("st" if k == KD - 1 else None))
            p.wait("vector", "st")
            p.emit("vector", lambda v, tc=tc: v.tensor_copy(
                arf[:, 2048 + tc * 8:2048 + (tc + 1) * 8], psf[5][:, 0:8]), inc="sv")
            p.wait("tensor", "sv")
        vdrain()
        lg3 = arf[:, 2048:2112].rearrange("p (a j) -> p a j", a=8)
        p.emit("vector", lambda v: v.tensor_reduce(
            stats[:, 40:48], lg3, axis=AX, op=ALU.max))
        vdrain()
        for tc in range(TC):
            p.emit("vector", lambda v, tc=tc: v.tensor_scalar(
                arf[:, 2112 + tc * 8:2112 + (tc + 1) * 8],
                arf[:, 2048 + tc * 8:2048 + (tc + 1) * 8],
                stats[:, 40 + tc:41 + tc], None, ALU.is_equal, ALU.bypass))
        vdrain()
        p.emit("vector", lambda v: v.tensor_scalar(
            arf[:, 2112:2176], arf[:, 2112:2176], -1e30, None, ALU.mult, ALU.bypass))
        vdrain()
        p.emit("vector", lambda v: v.tensor_tensor(
            arf[:, 2112:2176], arf[:, 2112:2176], arf[:, 2048:2112], ALU.add))
        vdrain()
        p.emit("vector", lambda v: v.tensor_reduce(
            stats[:, 48:56], arf[:, 2112:2176].rearrange("p (a j) -> p a j", a=8),
            axis=AX, op=ALU.max))
        vdrain()
        p.emit("vector", lambda v: v.tensor_tensor(
            stats[:, 32:40], stats[:, 40:48], stats[:, 48:56], ALU.add), inc="sv")
        for q in range(4):
            tok0 = q * 256
            p.barrier()
            slab_done = [0, 0]
            evac_done = [0, 0]
            sil_done = [0, 0]
            for isl in range(ISL):
                slot = isl % 2
                wof = A_WS + slot * 4096
                r, loc = isl // 8, isl % 8
                qs = "qa" if slot == 0 else "qb"
                if isl >= 2:
                    p.wait("sync", "st", slab_done[slot])
                    p.wait("tensor", "sv", evac_done[slot])
                p.emit("sync", lambda s, r=r, loc=loc, wof=wof: s.dma_start(
                    arena[:, wof:wof + 2048],
                    ag_out[r, OFF_1U + loc * SZ_T:OFF_1U + (loc + 1) * SZ_T].rearrange("(q2 x) -> q2 x", q2=128)),
                    inc=qs, n=16)
                p.wait("tensor", qs)
                bg = psf[slot]
                for k in range(KD):
                    p.emit("tensor", lambda t, k=k, wof=wof, bg=bg, tok0=tok0: t.matmul(
                        bg[:, 0:256], arena[:, wof + k * 128:wof + (k + 1) * 128],
                        xlnT[:, k * HALF + tok0:k * HALF + tok0 + 256],
                        start=(k == 0), stop=(k == KD - 1)), inc=("st" if k == KD - 1 else None))
                slab_done[slot] = p.val("st")
                itof = A_IT1 + isl * 256
                sof = 2048 + slot * 512
                p.wait("scalar", "st")
                if isl >= 2:
                    p.wait("scalar", "sv", sil_done[slot])
                p.emit("scalar", lambda s, bg=bg, sof=sof: s.activation(
                    arf[:, sof:sof + 256], bg[:, 0:256], AF.Relu), inc="ss")
                p.wait("vector", "ss")
                p.emit("vector", lambda v, itof=itof, sof=sof: v.tensor_tensor(
                    arena[:, itof:itof + 256], arf[:, sof:sof + 256],
                    arf[:, sof:sof + 256], ALU.mult), inc="sv")
                evac_done[slot] = p.val("sv")
                sil_done[slot] = p.val("sv")
            p.barrier()
            slab_done = [0, 0]
            for ns in range(4):
                if ns > 0:
                    p.wait("tensor", "sv", p.val("sv"))
                for rg in range(8):
                    slot = (ns * 8 + rg) % 2
                    dwf = A_DW + slot * 4096
                    qs = "qc" if slot == 0 else "qd"
                    if ns * 8 + rg >= 2:
                        p.wait("sync", "st", slab_done[slot])
                    p.emit("sync", lambda s, rg=rg, ns=ns, dwf=dwf: s.dma_start(
                        arena[:, dwf:dwf + 4096].rearrange("p (a x) -> p a x", a=8),
                        ag_out[rg, OFF_1D:OFF_1D + 8 * SZ_T].rearrange(
                            "(a q2 dd) -> q2 a dd", a=8, q2=128)[:, :, ns * 512:(ns + 1) * 512]),
                        inc=qs, n=16)
                    p.wait("tensor", qs)
                    for tq in range(2):
                        bank = psf[2 + tq]
                        for loc in range(8):
                            ic = rg * 8 + loc
                            last = (rg == 7 and loc == 7)
                            bump = (tq == 1 and loc == 7) or last
                            p.emit("tensor", lambda t, ic=ic, loc=loc, dwf=dwf, bank=bank, tq=tq: t.matmul(
                                bank[:, 0:512],
                                arena[:, A_IT1 + ic * 256 + tq * 128:A_IT1 + ic * 256 + (tq + 1) * 128],
                                arena[:, dwf + loc * 512:dwf + (loc + 1) * 512],
                                start=(ic == 0), stop=last, skip_group_check=True),
                                inc=("st" if bump else None))
                    slab_done[slot] = p.val("st")
                p.wait("vector", "st")
                for tq in range(2):
                    p.emit("vector", lambda v, tq=tq, ns=ns: v.tensor_copy(
                        arena[:, A_Y1 + tq * 2048 + ns * 512:A_Y1 + tq * 2048 + (ns + 1) * 512],
                        psf[2 + tq][:, 0:512]), inc="sv")
            p.barrier()
            for nsp in range(2):
                if nsp > 0:
                    p.wait("tensor", "sv", p.val("sv"))
                slabd = [0, 0]
                for k in range(KD):
                    slot = k % 2
                    mwf = A_MW1 + slot * 2048
                    qs = "qc" if slot == 0 else "qd"
                    if k >= 2 or nsp > 0:
                        p.wait("sync", "st", slabd[slot] if k >= 2 else p.val("st"))
                    p.emit("sync", lambda s, k=k, mwf=mwf: s.dma_start(
                        arena[:, mwf:mwf + 2048],
                        ag_out[k // 2, OFF_1E + (k % 2) * SZ_T:OFF_1E + (k % 2 + 1) * SZ_T].rearrange(
                            "(q2 x) -> q2 x", q2=128)), inc=qs, n=16)
                    p.wait("tensor", qs)
                    for ns2 in range(2):
                        ns = nsp * 2 + ns2
                        for tq in range(2):
                            last = (k == KD - 1)
                            bump = (ns2 == 1 and tq == 1)
                            p.emit("tensor", lambda t, k=k, mwf=mwf, ns=ns, ns2=ns2, tq=tq, tok0=tok0, last=last: t.matmul(
                                psf[2 + ns2 * 2 + tq][:, 0:512],
                                xlnT[:, (k * HALF + tok0 + tq * 128):(k * HALF + tok0 + (tq + 1) * 128)],
                                arena[:, mwf + ns * 512:mwf + (ns + 1) * 512],
                                start=(k == 0), stop=last, skip_group_check=True),
                                inc=("st" if bump else None))
                    slabd[slot] = p.val("st")
                p.wait("vector", "st")
                for ns2 in range(2):
                    ns = nsp * 2 + ns2
                    for tq in range(2):
                        tc = q * 2 + tq
                        p.emit("vector", lambda v, ns=ns, ns2=ns2, tq=tq, tc=tc: v.scalar_tensor_tensor(
                            arena[:, A_Y1 + tq * 2048 + ns * 512:A_Y1 + tq * 2048 + (ns + 1) * 512],
                            psf[2 + ns2 * 2 + tq][:, 0:512], stats[:, 32 + tc:33 + tc],
                            arena[:, A_Y1 + tq * 2048 + ns * 512:A_Y1 + tq * 2048 + (ns + 1) * 512],
                            ALU.mult, ALU.add), inc="sv")
            p.barrier()
            for tq in range(2):
                tc = q * 2 + tq
                yq = arena[:, A_Y1 + tq * 2048:A_Y1 + (tq + 1) * 2048]
                sqo = A_YN if tq == 0 else A_MW1
                dmo = A_T1 if tq == 0 else A_MW1 + 2048
                p.emit("vector", lambda v, yq=yq, tq=tq: v.tensor_reduce(
                    stats[:, 56 + tq:57 + tq], yq, axis=AX, op=ALU.add))
                p.emit("scalar", lambda s, yq=yq, sqo=sqo: s.activation(
                    arena[:, sqo:sqo + 2048], yq, AF.Square), inc="ss")
                p.wait("vector", "ss")
                p.emit("vector", lambda v, tq=tq, sqo=sqo, dmo=dmo: v.tensor_scalar(
                    arena[:, dmo:dmo + 2048], arena[:, sqo:sqo + 2048],
                    1.0 / D, None, ALU.mult, ALU.add, accum_out=stats[:, 58 + tq:59 + tq]), inc="sv")
            vdrain()
            p.emit("vector", lambda v: v.tensor_scalar(
                stats[:, 60:62], stats[:, 56:58], 1.0 / D, None, ALU.mult, ALU.bypass))
            vdrain()
            p.emit("vector", lambda v: v.tensor_tensor(
                stats[:, 62:64], stats[:, 60:62], stats[:, 60:62], ALU.mult))
            vdrain()
            p.emit("vector", lambda v: v.tensor_tensor(
                stats[:, 62:64], stats[:, 58:60], stats[:, 62:64], ALU.subtract))
            vdrain()
            p.emit("vector", lambda v: v.tensor_scalar(
                stats[:, 62:64], stats[:, 62:64], 1e-5, None, ALU.add, ALU.bypass), inc="sv")
            p.wait("scalar", "sv")
            p.emit("scalar", lambda s: s.activation(
                stats[:, 56:58], stats[:, 62:64], AF.Sqrt), inc="ss")
            p.wait("vector", "ss")
            p.emit("vector", lambda v: v.reciprocal(stats[:, 58:60], stats[:, 56:58]))
            vdrain()
            p.emit("vector", lambda v: v.tensor_scalar(
                stats[:, 58:60], stats[:, 58:60], 0.5, None, ALU.mult, ALU.bypass), inc="sv")
            vdrain()
            p.emit("sync", lambda s: s.dma_start(arf[0:1, 0:2048], dram["f1lng"]), inc="sl", n=16)
            p.wait("tensor", "sl")
            for tq in range(2):
                tc = q * 2 + tq
                yq = arena[:, A_Y1 + tq * 2048:A_Y1 + (tq + 1) * 2048]
                p.wait("vector", "sl")
                p.emit("vector", lambda v, yq=yq, tq=tq: v.tensor_scalar(
                    arena[:, A_YN:A_YN + 2048], yq,
                    stats[:, 60 + tq:61 + tq], stats[:, 58 + tq:59 + tq],
                    ALU.subtract, ALU.mult), inc="sv")
                for colh in range(2):
                    p.wait("sync", "st", p.val("st"))
                    p.wait("sync", "sv", p.val("sv"))
                    p.emit("sync", lambda s, colh=colh: s.dma_start(
                        arf[0:1, 2048:3072], dram["f1lnb"][0:1, colh * 1024:(colh + 1) * 1024]),
                        inc="sl", n=16)
                    p.wait("tensor", "sl")
                    p.wait("tensor", "sv", p.val("sv"))
                    for j in range(2):
                        p.emit("tensor", lambda t, colh=colh, j=j: t.matmul(
                            psf[j][:, 0:512], onesr_f[:],
                            arf[0:1, colh * 1024 + j * 512:colh * 1024 + (j + 1) * 512],
                            start=True, stop=True))
                        p.emit("tensor", lambda t, j=j: t.matmul(
                            psf[2 + j][:, 0:512], halfr_f[:],
                            arf[0:1, 2048 + j * 512:2048 + (j + 1) * 512],
                            start=True, stop=True), inc=("st" if j == 1 else None))
                    p.wait("vector", "st")
                    for j in range(2):
                        p.emit("vector", lambda v, colh=colh, j=j: v.tensor_tensor(
                            arena[:, A_T1 + j * 512:A_T1 + (j + 1) * 512],
                            arena[:, A_YN + colh * 1024 + j * 512:A_YN + colh * 1024 + (j + 1) * 512],
                            psf[j][:, 0:512], ALU.mult))
                    vdrain()
                    for j in range(2):
                        p.emit("vector", lambda v, j=j: v.tensor_tensor(
                            arena[:, A_T1 + 1024 + j * 512:A_T1 + 1024 + (j + 1) * 512],
                            arena[:, A_T1 + j * 512:A_T1 + (j + 1) * 512],
                            psf[2 + j][:, 0:512], ALU.add))
                    vdrain()
                    p.emit("vector", lambda v, tc=tc, colh=colh: v.tensor_tensor(
                        R[:, tc * D + colh * 1024:tc * D + (colh + 1) * 1024],
                        R[:, tc * D + colh * 1024:tc * D + (colh + 1) * 1024],
                        arena[:, A_T1 + 1024:A_T1 + 2048], ALU.add), inc="sv")
                    vdrain()
        p.barrier()


    A_QP = 2048
    A_KP = 3072
    A_KT = 4096
    A_VT = 5120
    A_SC2 = 6144
    A_YS2 = 14336
    A_GW2 = 15360
    A_GB2 = 17408
    A_XP2 = 19456
    A_WR = 21504
    A_GZ2 = 25600
    A_SB2 = 41984
    SHUF_PAIR = []
    for qd_ in range(16):
        SHUF_PAIR += [2 * qd_ + 1, 2 * qd_]

    def retention_phase():
        ln_phase()
        p.barrier()
        for nm, dst in (("gnw_row", A_GW2), ("gnb_row", A_GB2)):
            p.emit("sync", lambda s, nm=nm: s.dma_start(arf[0:1, 0:2048], dram[nm]), inc="sl", n=16)
            p.wait("tensor", "sl")
            p.wait("tensor", "sv", p.val("sv"))
            for j in range(4):
                p.emit("tensor", lambda t, j=j: t.matmul(
                    psf[j][:, 0:512], onesr_f[:], arf[0:1, j * 512:(j + 1) * 512],
                    start=True, stop=True), inc=("st" if j == 3 else None))
            p.wait("vector", "st")
            for j in range(4):
                p.emit("vector", lambda v, j=j, dst=dst: v.tensor_copy(
                    arena[:, dst + j * 512:dst + (j + 1) * 512], psf[j][:, 0:512]), inc="sv")
            p.wait("sync", "sv")
        ccv = {}
        for h in range(HR):
            for wi, wname in enumerate(("wq_t", "wk_t", "wv_t")):
                slot = wi % 2
                wof = A_WR + slot * 2048
                qs = "ga" if slot == 0 else "gb"
                p.wait("gpsimd", "st", p.val("st"))
                p.emit("gpsimd", lambda g, h=h, wname=wname, wof=wof: g.dma_start(
                    arena[:, wof:wof + 2048], dram[wname][h]), inc=qs, n=16)
                p.wait("tensor", qs)
                p.wait("tensor", "sv", p.val("sv"))
                for hf in range(2):
                    bk = psf[hf]
                    for k in range(KD):
                        p.emit("tensor", lambda t, k=k, wof=wof, bk=bk, hf=hf: t.matmul(
                            bk[:, 0:512], arena[:, wof + k * 128:wof + (k + 1) * 128],
                            xlnT[:, k * HALF + hf * 512:k * HALF + (hf + 1) * 512],
                            start=(k == 0), stop=(k == KD - 1)),
                            inc=("st" if (k == KD - 1 and hf == 1) else None))
                if wi < 2:
                    dst = A_QP if wi == 0 else A_KP
                    cnm = "xq_c" if wi == 0 else "xk_c"
                    snm = "xq_s" if wi == 0 else "xk_s"
                    p.wait("gpsimd", "sv", p.val("sv"))
                    p.emit("gpsimd", lambda g, h=h, cnm=cnm: g.dma_start(
                        arena[:, A_XP2:A_XP2 + 1024], dram[cnm][:, h * 1024:(h + 1) * 1024]),
                        inc="gc", n=16)
                    p.emit("gpsimd", lambda g, h=h, snm=snm: g.dma_start(
                        arena[:, A_XP2 + 1024:A_XP2 + 2048], dram[snm][:, h * 1024:(h + 1) * 1024]),
                        inc="gc", n=16)
                    p.wait("vector", "gc")
                    p.wait("vector", "st")
                    for hf in range(2):
                        bk = psf[hf]
                        p.emit("vector", lambda v, bk=bk, hf=hf, dst=dst: v.tensor_tensor(
                            arena[:, A_KT + hf * 512:A_KT + (hf + 1) * 512], bk[:, 0:512],
                            arena[:, A_XP2 + hf * 512:A_XP2 + (hf + 1) * 512], ALU.mult))
                        p.emit("vector", lambda v, bk=bk, hf=hf: v.stream_shuffle(
                            arf[:, hf * 512:(hf + 1) * 512], bk[:, 0:512], SHUF_PAIR))
                    vdrain()
                    for hf in range(2):
                        p.emit("vector", lambda v, hf=hf, dst=dst: v.tensor_tensor(
                            arena[:, dst + hf * 512:dst + (hf + 1) * 512],
                            arf[:, hf * 512:(hf + 1) * 512],
                            arena[:, A_XP2 + 1024 + hf * 512:A_XP2 + 1024 + (hf + 1) * 512],
                            ALU.mult))
                    vdrain()
                    for hf in range(2):
                        p.emit("vector", lambda v, hf=hf, dst=dst: v.tensor_tensor(
                            arena[:, dst + hf * 512:dst + (hf + 1) * 512],
                            arena[:, dst + hf * 512:dst + (hf + 1) * 512],
                            arena[:, A_KT + hf * 512:A_KT + (hf + 1) * 512], ALU.add), inc="sv")
                    vdrain()
                else:
                    p.wait("vector", "st")
                    for hf in range(2):
                        bk = psf[hf]
                        p.emit("vector", lambda v, bk=bk, hf=hf: v.tensor_copy(
                            arena[:, A_KT + hf * 512:A_KT + (hf + 1) * 512], bk[:, 0:512]), inc="sv")
                    vdrain()
            p.wait("tensor", "sv", p.val("sv"))
            for sk in range(8):
                p.emit("tensor", lambda t, sk=sk: t.transpose(
                    psb[0][:, (sk % 4) * 128:(sk % 4 + 1) * 128],
                    arena[:, A_KT + sk * 128:A_KT + (sk + 1) * 128], identb[:]), inc="st")
                p.wait("vector", "st")
                p.emit("vector", lambda v, sk=sk: v.tensor_copy(
                    arena[:, A_VT + sk * 128:A_VT + (sk + 1) * 128],
                    psb[0][:, (sk % 4) * 128:(sk % 4 + 1) * 128]), inc="sv")
                p.wait("tensor", "sv")
            for sk in range(8):
                p.emit("tensor", lambda t, sk=sk: t.transpose(
                    psb[0][:, (sk % 4) * 128:(sk % 4 + 1) * 128],
                    arena[:, A_KP + sk * 128:A_KP + (sk + 1) * 128], identb[:]), inc="st")
                p.wait("vector", "st")
                p.emit("vector", lambda v, sk=sk: v.tensor_copy(
                    arena[:, A_KT + sk * 128:A_KT + (sk + 1) * 128],
                    psb[0][:, (sk % 4) * 128:(sk % 4 + 1) * 128]), inc="sv")
                p.wait("tensor", "sv")
            for sk in range(8):
                p.emit("tensor", lambda t, sk=sk: t.matmul(
                    psf[2][:, 0:128], arena[:, A_KT + sk * 128:A_KT + (sk + 1) * 128],
                    arena[:, A_VT + sk * 128:A_VT + (sk + 1) * 128],
                    start=(sk == 0), stop=(sk == 7)), inc=("st" if sk == 7 else None))
            p.wait("vector", "st")
            p.emit("vector", lambda v: v.tensor_copy(arena[:, A_SB2:A_SB2 + 128], psf[2][:, 0:128]), inc="sv")
            p.wait("gpsimd", "sv")
            p.emit("gpsimd", lambda g, h=h: g.dma_start(rb_in[h][:, :], arena[:, A_SB2:A_SB2 + 128]),
                   inc="go", n=16)
            p.wait("gpsimd", "go")
            p.emit("gpsimd", lambda g, h=h: g.collective_compute(
                "AllGather", ALU.bypass, replica_groups=[[0, 1], [2, 3], [4, 5], [6, 7]],
                ins=[rb_in[h].opt()], outs=[rb_out[h].opt()]), inc="cc")
            ccv[h] = p.val("cc")
            p.wait("tensor", "sv", p.val("sv"))
            for sk in range(8):
                b0 = 0 if sk < 4 else 1
                for b_ in range(b0, 2):
                    p.emit("tensor", lambda t, sk=sk, b_=b_: t.matmul(
                        psf[3][:, 0:512], arena[:, A_KP + sk * 128:A_KP + (sk + 1) * 128],
                        arena[:, A_QP + b_ * 512:A_QP + (b_ + 1) * 512],
                        start=True, stop=True), inc="st")
                    p.wait("vector", "st")
                    p.emit("vector", lambda v, sk=sk, b_=b_: v.tensor_copy(
                        arena[:, A_SC2 + sk * 1024 + b_ * 512:A_SC2 + sk * 1024 + (b_ + 1) * 512],
                        psf[3][:, 0:512]), inc="sv")
                    if b_ == b0:
                        off = sk * 128 - b_ * 512
                        vdrain()
                        p.emit("vector", lambda v, sk=sk, off=off, b_=b_: v.tensor_tensor(
                            arena[:, A_SC2 + sk * 1024 + sk * 128:A_SC2 + sk * 1024 + (sk + 1) * 128],
                            psf[3][:, off:off + 128], maskr[:], ALU.mult), inc="sv")
                    vdrain()
                    p.wait("tensor", "sv", p.val("sv"))
            p.wait("gpsimd", "cc", ccv[h])
            p.wait("gpsimd", "sv", p.val("sv"))
            p.emit("gpsimd", lambda g, h=h: g.dma_start(Tld[:], rb_out[h][0]), inc="gc", n=16)
            p.wait("vector", "gc")
            p.emit("vector", lambda v: v.tensor_scalar(
                arena[:, A_SB2 + 128:A_SB2 + 256], Tld[:], wselc[:], None, ALU.mult, ALU.bypass),
                inc="sv")
            p.wait("tensor", "sv", p.val("sv"))
            for tc in range(TC):
                for sk in range(tc + 1):
                    p.emit("tensor", lambda t, tc=tc, sk=sk: t.matmul(
                        psf[4][:, 0:128],
                        arena[:, A_SC2 + sk * 1024 + tc * 128:A_SC2 + sk * 1024 + (tc + 1) * 128],
                        arena[:, A_VT + sk * 128:A_VT + (sk + 1) * 128],
                        start=(sk == 0), stop=False, skip_group_check=True))
                p.emit("tensor", lambda t, tc=tc: t.matmul(
                    psf[4][:, 0:128], arena[:, A_QP + tc * 128:A_QP + (tc + 1) * 128],
                    arena[:, A_SB2 + 128:A_SB2 + 256],
                    start=False, stop=True, skip_group_check=True), inc="st")
                p.wait("vector", "st")
                p.emit("vector", lambda v, tc=tc: v.tensor_copy(
                    arena[:, A_YS2 + tc * 128:A_YS2 + (tc + 1) * 128], psf[4][:, 0:128]), inc="sv")
                p.wait("tensor", "sv", p.val("sv"))
            vdrain()
            y3 = arena[:, A_YS2:A_YS2 + 1024].rearrange("p (a j) -> p a j", a=8)
            p.emit("vector", lambda v, y3=y3: v.tensor_reduce(
                stats[:, 48:56], y3, axis=AX, op=ALU.add))
            p.wait("scalar", "sv", p.val("sv"))
            p.wait("scalar", "ss", p.val("ss"))
            p.emit("scalar", lambda s: s.activation(
                arf[:, 0:1024], arena[:, A_YS2:A_YS2 + 1024], AF.Square), inc="ss")
            p.wait("vector", "ss")
            p.emit("vector", lambda v: v.tensor_reduce(
                stats[:, 56:64], arf[:, 0:1024].rearrange("p (a j) -> p a j", a=8),
                axis=AX, op=ALU.add))
            vdrain()
            p.emit("vector", lambda v: v.tensor_scalar(
                stats[:, 48:56], stats[:, 48:56], 1.0 / 128, None, ALU.mult, ALU.bypass))
            p.emit("vector", lambda v: v.tensor_scalar(
                stats[:, 56:64], stats[:, 56:64], 1.0 / 128, None, ALU.mult, ALU.bypass))
            vdrain()
            p.emit("vector", lambda v: v.tensor_tensor(
                stats[:, 32:40], stats[:, 48:56], stats[:, 48:56], ALU.mult))
            vdrain()
            p.emit("vector", lambda v: v.tensor_tensor(
                stats[:, 56:64], stats[:, 56:64], stats[:, 32:40], ALU.subtract))
            vdrain()
            p.emit("vector", lambda v: v.tensor_scalar(
                stats[:, 56:64], stats[:, 56:64], 1e-5, None, ALU.add, ALU.bypass), inc="sv")
            p.wait("scalar", "sv")
            p.emit("scalar", lambda s: s.activation(stats[:, 32:40], stats[:, 56:64], AF.Sqrt), inc="ss")
            p.wait("vector", "ss")
            p.emit("vector", lambda v: v.reciprocal(stats[:, 56:64], stats[:, 32:40]))
            vdrain()
            for tc in range(TC):
                p.emit("vector", lambda v, tc=tc: v.tensor_scalar(
                    arena[:, A_KT + tc * 128:A_KT + (tc + 1) * 128],
                    arena[:, A_YS2 + tc * 128:A_YS2 + (tc + 1) * 128],
                    stats[:, 48 + tc:49 + tc], stats[:, 56 + tc:57 + tc],
                    ALU.subtract, ALU.mult))
            vdrain()
            for tc in range(TC):
                p.emit("vector", lambda v, tc=tc, h=h: v.tensor_tensor(
                    arena[:, A_VT + tc * 128:A_VT + (tc + 1) * 128],
                    arena[:, A_KT + tc * 128:A_KT + (tc + 1) * 128],
                    arena[:, A_GW2 + h * 128:A_GW2 + (h + 1) * 128], ALU.mult))
            vdrain()
            for tc in range(TC):
                p.emit("vector", lambda v, tc=tc, h=h: v.tensor_tensor(
                    arena[:, A_GZ2 + h * 1024 + tc * 128:A_GZ2 + h * 1024 + (tc + 1) * 128],
                    arena[:, A_VT + tc * 128:A_VT + (tc + 1) * 128],
                    arena[:, A_GB2 + h * 128:A_GB2 + (h + 1) * 128], ALU.add), inc="sv")
            vdrain()
        p.barrier()
        for ns in range(4):
            for tcg in range(2):
                if ns + tcg > 0:
                    p.wait("tensor", "sv", p.val("sv"))
                slabd = [0, 0]
                for k in range(KD):
                    slot = k % 2
                    wof = A_WR + slot * 2048
                    qs = "ga" if slot == 0 else "gb"
                    if k >= 2 or ns + tcg > 0:
                        p.wait("gpsimd", "st", slabd[slot] if k >= 2 else p.val("st"))
                    p.emit("gpsimd", lambda g, k=k, ns=ns, wof=wof: g.dma_start(
                        arena[:, wof:wof + 512],
                        dram["wg"][k * 128:(k + 1) * 128, ns * 512:(ns + 1) * 512]), inc=qs, n=16)
                    p.wait("tensor", qs)
                    for tq in range(4):
                        tc = tcg * 4 + tq
                        last = (k == KD - 1)
                        bump = (tq == 3)
                        p.emit("tensor", lambda t, k=k, wof=wof, tq=tq, tc=tc, last=last: t.matmul(
                            psf[2 + tq][:, 0:512],
                            xlnT[:, k * HALF + tc * 128:k * HALF + (tc + 1) * 128],
                            arena[:, wof:wof + 512],
                            start=(k == 0), stop=last, skip_group_check=True),
                            inc=("st" if bump else None))
                    slabd[slot] = p.val("st")
                p.wait("scalar", "st")
                p.wait("scalar", "sv", p.val("sv"))
                for tq in range(4):
                    tc = tcg * 4 + tq
                    p.emit("scalar", lambda s, tq=tq: s.activation(
                        arf[:, tq * 512:(tq + 1) * 512], psf[2 + tq][:, 0:512], AF.Silu), inc="ss")
                p.wait("vector", "ss")
                for tq in range(4):
                    tc = tcg * 4 + tq
                    yn3 = arena[:, A_GZ2:A_GZ2 + 16384].rearrange(
                        "p (a x) -> p a x", a=16)[:, ns * 4:(ns + 1) * 4, tc * 128:(tc + 1) * 128]
                    s3 = arf[:, tq * 512:(tq + 1) * 512].rearrange("p (a j) -> p a j", a=4)
                    p.emit("vector", lambda v, yn3=yn3, s3=s3: v.tensor_tensor(
                        yn3, yn3, s3, ALU.mult), inc="sv")
        p.barrier()
        for h in range(HR):
            for tc in range(TC):
                off = A_GZ2 + h * 1024 + tc * 128
                p.emit("tensor", lambda t, off=off: t.transpose(
                    psb[0][:, 0:128], arena[:, off:off + 128], identb[:]), inc="st")
                p.wait("vector", "st")
                p.emit("vector", lambda v, off=off: v.tensor_copy(
                    arena[:, off:off + 128], psb[0][:, 0:128]), inc="sv")
                p.wait("tensor", "sv")
        p.barrier()
        for hf in range(2):
            for ns in range(4):
                if hf + ns > 0:
                    p.wait("tensor", "sv", p.val("sv"))
                for g in range(HR):
                    slot = g % 2
                    qs = "ga" if slot == 0 else "gb"
                    wof = A_WR + slot * 2048
                    if g >= 2 or hf + ns > 0:
                        p.wait("gpsimd", "st", p.val("st"))
                    p.emit("gpsimd", lambda gg, g=g, ns=ns, wof=wof: gg.dma_start(
                        arena[:, wof:wof + 512],
                        dram["wo"][g * 128:(g + 1) * 128, ns * 512:(ns + 1) * 512]), inc=qs, n=16)
                    p.wait("tensor", qs)
                    for tq in range(4):
                        last = (g == HR - 1)
                        p.emit("tensor", lambda t, g=g, hf=hf, tq=tq, wof=wof, last=last: t.matmul(
                            psf[2 + tq][:, 0:512],
                            arena[:, A_GZ2 + g * 1024 + hf * 512 + tq * 128:
                                   A_GZ2 + g * 1024 + hf * 512 + (tq + 1) * 128],
                            arena[:, wof:wof + 512],
                            start=(g == 0), stop=last, skip_group_check=True),
                            inc=("st" if tq == 3 else None))
                p.wait("vector", "st")
                for tq in range(4):
                    tc = hf * 4 + tq
                    p.emit("vector", lambda v, tc=tc, ns=ns, tq=tq: v.tensor_tensor(
                        R[:, tc * D + ns * 512:tc * D + (ns + 1) * 512],
                        R[:, tc * D + ns * 512:tc * D + (ns + 1) * 512],
                        psf[2 + tq][:, 0:512], ALU.add), inc="sv")
        p.barrier()

    def dump_dbg():
        p.barrier()
        p.emit("vector", lambda v: v.tensor_copy(arf[:, 0:1024], xlnT[:, 0:1024]), inc="sv")
        p.wait("sync", "sv")
        p.emit("sync", lambda s: s.dma_start(dbg_t[:, 0:1024], arf[:, 0:1024]), inc="so", n=16)
        p.emit("sync", lambda s: s.dma_start(
            arena[:, 0:2048], ag_out[0, OFF_G:OFF_G + SZ_T].rearrange("(q x) -> q x", q=128)),
            inc="sl", n=16)
        p.wait("vector", "sl")
        p.wait("vector", "so")
        p.emit("vector", lambda v: v.tensor_copy(arf[:, 0:2048], arena[:, 0:2048]), inc="sv")
        p.wait("sync", "sv")
        p.emit("sync", lambda s: s.dma_start(dbg_t[:, 1024:3072], arf[:, 0:2048]), inc="so", n=16)
        p.wait("vector", "so")
        p.emit("vector", lambda v: v.tensor_copy(arf[:, 0:512], arena[:, A_IT:A_IT + 512]), inc="sv")
        p.emit("vector", lambda v: v.tensor_copy(arf[:, 512:1024], arena[:, A_IT + 63 * 512:A_IT + 64 * 512]), inc="sv")
        p.wait("sync", "sv")
        p.emit("sync", lambda s: s.dma_start(dbg_t[:, 3072:4096], arf[:, 0:1024]), inc="so", n=16)
        p.wait("vector", "so")
        p.emit("vector", lambda v: v.tensor_copy(arf[:, 0:512], arena[:, A_IT + 8 * 512:A_IT + 9 * 512]), inc="sv")
        p.emit("vector", lambda v: v.tensor_copy(arf[:, 512:1024], arena[:, A_IT + 56 * 512:A_IT + 57 * 512]), inc="sv")
        p.wait("sync", "sv")
        p.emit("sync", lambda s: s.dma_start(dbg_t[:, 4096:5120], arf[:, 0:1024]), inc="so", n=16)
        p.emit("sync", lambda s: s.dma_start(
            arena[:, 0:2048], ag_out[1, OFF_G:OFF_G + SZ_T].rearrange("(q x) -> q x", q=128)),
            inc="sl", n=16)
        p.wait("vector", "sl")
        p.wait("vector", "so")
        p.emit("vector", lambda v: v.tensor_copy(arf[:, 0:2048], arena[:, 0:2048]), inc="sv")
        p.wait("sync", "sv")
        p.emit("sync", lambda s: s.dma_start(dbg_t[:, 5120:7168], arf[:, 0:2048]), inc="so", n=16)
        p.wait("gpsimd", "so")

    if N_SUB >= 1:
        ffn0_call()
        if os.environ.get("BASS_DBG"):
            dump_dbg()
    if N_SUB >= 2:
        mamba_phase()
    if N_SUB >= 3:
        ffn0_call()
    if N_SUB >= 4:
        ffn1_call()
    if N_SUB >= 5:
        retention_phase()
    if N_SUB >= 6:
        ffn1_call()
    store_out()

    block = ctx.enter_context(nc.Block())
    p.replay(block)
    ctx.close()
    return nc


_NC_CACHE = {}


def kernel(**inputs):
    ins = {k: np.asarray(v) for k, v in inputs.items()}
    shared, per_core = _host_prep(ins)
    key = N_SUB
    if key not in _NC_CACHE:
        _NC_CACHE[key] = build_nc()
    nc = _NC_CACHE[key]
    in_maps = []
    for c in range(8):
        m = dict(shared)
        m.update(per_core[c])
        in_maps.append(m)
    res = run_bass_kernel_spmd(nc, in_maps, core_ids=list(range(8)))
    out = np.empty((B, S, D), np.float32)
    for c in range(8):
        b, hh = c // 2, c % 2
        out[b, hh * 1024:(hh + 1) * 1024, :] = np.asarray(res.results[c]["out"])
    if os.environ.get("BASS_DBG"):
        kernel._dbg = [np.asarray(res.results[c].get("dbg")) for c in range(8)]
    return out
